# version 20
# baseline (speedup 1.0000x reference)
import numpy as np
import ml_dtypes

import concourse.bass as bass
import concourse.mybir as mybir
from concourse.bass_utils import run_bass_kernel_spmd

F32 = mybir.dt.float32
BF16 = mybir.dt.bfloat16
AF = mybir.ActivationFunctionType
ALU = mybir.AluOpType
AX = mybir.AxisListType.X

D = 2048
T = 4096
NH = 16
NKV = 4
DK = 128
E = 32
FI = 768
TS = 512
TB = 128
KB = D // 128
G = 4
NG = E // G
INVD = 1.0 / D
EPS = 1e-5

DBG = 0


class Rec:
    ENGS = ("y", "g", "t", "v", "s")

    def __init__(self, nc):
        self.nc = nc
        self.ev = {e: [] for e in self.ENGS}
        self.ctr = {}
        self.sems = {}

    def mksem(self, *names):
        for n in names:
            self.sems[n] = self.nc.semaphore(n).__enter__()
            self.ctr[n] = 0

    def do(self, eng, fn, inc=None, n=None):
        tgt = None
        if inc is not None:
            if n is None:
                n = 16 if inc.startswith("d_") else 1
            self.ctr[inc] += n
            tgt = (self.sems[inc], n)
        self.ev[eng].append(("op", fn, tgt))
        return self.ctr[inc] if inc else None

    def wait(self, eng, sem, val=None):
        v = self.ctr[sem] if val is None else val
        if v > 0:
            self.ev[eng].append(("wait", self.sems[sem], v))

    def switch(self, eng, fn):
        self.ev[eng].append(("switch", fn, None))

    def barrier(self):
        snap = dict(self.ctr)
        for e in self.ENGS:
            for s, v in snap.items():
                if v > 0:
                    self.ev[e].append(("wait", self.sems[s], v))

    def emit(self, eng_name, e):
        for kind, a, b in self.ev[eng_name]:
            if kind == "op":
                inst = a(e)
                if b is not None:
                    inst.then_inc(b[0], b[1])
            elif kind == "wait":
                e.wait_ge(a, b)
            else:
                for arm in self.nc.switch_core_id([e], n=128):
                    a(e, arm.logical % 8)


def build_program():
    nc = bass.Bass("TRN2", target_bir_lowering=False)
    r = Rec(nc)

    def ein(name, shape, dt):
        return nc.dram_tensor(name, list(shape), dt, kind="ExternalInput").ap()

    def din(name, shape, dt):
        return nc.dram_tensor(name, list(shape), dt, kind="Internal").ap()

    x0d = ein("x0", (T, D), F32)
    qwt_d = ein("qwt", (D, D), F32)
    kvwt_d = ein("kvwt", (D, 2 * NKV * DK), F32)
    cwt_d = ein("cwt", (D, D), F32)
    gate_d = ein("gatew", (D, E), F32)
    gup_d = ein("gup", (E * D, 2 * FI), F32)
    dwn_d = ein("dwn", (E * FI, D), F32)
    cosq_d = ein("cosq", (64, T), F32)
    sinq_d = ein("sinq", (64, T), F32)
    cosk_d = ein("cosk", (64, T), F32)
    sink_d = ein("sink", (64, T), F32)
    tri_d = ein("tri", (128, 4 * 512), F32)
    consts_d = ein("consts", (128, 8), F32)
    iden32_d = ein("iden32", (128, 128), F32)

    outd = nc.dram_tensor("out", [4 * TS, D], F32, kind="ExternalOutput").ap()
    dbg_out = {}
    if DBG >= 1:
        dbg_out["x1"] = nc.dram_tensor("dbg_x1", [T, D], BF16,
                                       kind="ExternalOutput").ap()
        dbg_out["xn2"] = nc.dram_tensor("dbg_xn2", [T, D], BF16,
                                        kind="ExternalOutput").ap()
    if DBG >= 2:
        dbg_out["attn"] = nc.dram_tensor("dbg_attn", [D, T], BF16,
                                         kind="ExternalOutput").ap()
        dbg_out["x2"] = nc.dram_tensor("dbg_x2", [4 * TS, D], BF16,
                                       kind="ExternalOutput").ap()

    gup_bf = din("gup_bf", (E * 2 * 128, KB * 768), BF16)
    dwn_bf = din("dwn_bf", (E * 4 * 128, 6 * 512), BF16)
    qwt_bf = din("qwt_bf", (4 * 128, KB * 512), BF16)
    kvwt_bf = din("kvwt_bf", (4 * 128, KB * 256), BF16)
    cwt_bf = din("cwt_bf", (128, KB * D), BF16)
    gate_bfd = din("gate_bfd", (128, KB * E), BF16)
    x1_hbm = din("x1_hbm", (T, D), BF16)
    xn2_hbm = din("xn2_hbm", (T, D), BF16)
    x2_hbm = din("x2_hbm", (4 * TS, D), BF16)
    xnt_hbm = [din(f"xnt_hbm{i}", (TS, D), BF16) for i in range(2)]
    accb_hbm = [din(f"accb_hbm{i}", (D, TS), BF16) for i in range(2)]
    wt_hbm = din("wt_hbm", (E, TS), BF16)
    p_hbm = [din(f"p_hbm{i}", (TS, T), BF16) for i in range(2)]
    attn_hbm = din("attn_hbm", (D, T), BF16)
    attnmy_hbm = din("attnmy_hbm", (D, 4 * TS), BF16)
    x1my_hbm = din("x1my_hbm", (4 * TS, D), BF16)

    r.mksem("d_conv", "d_x", "d_bnc", "d_xnt", "d_w1a", "d_w1b",
            "d_w2a", "d_w2b", "d_wb", "d_res", "d_p0", "d_p1",
            "d_pt0", "d_pt1", "d_att", "d_aw", "v_m", "sc_m", "t_m",
            "d_x2a", "d_x2b")

    consts_sb = nc.alloc_sbuf_tensor("consts_sb", [128, 8], F32)
    gate_sb = nc.alloc_sbuf_tensor("gate_sb2", [128, KB * E], BF16)
    iden32_sb = nc.alloc_sbuf_tensor("iden32_sb", [128, 128], F32)
    c_invD = consts_sb[:, 0:1]
    c_eps = consts_sb[:, 1:2]

    r.do("y", lambda e: e.dma_start(out=consts_sb[:], in_=consts_d[:]), "d_att")
    r.do("y", lambda e: e.dma_start(out=iden32_sb[:], in_=iden32_d[:]), "d_att")

    convmark = {}
    r.do("g", lambda e: e.dma_start(
        out=gate_bfd.rearrange("p (k c) -> k p c", c=E),
        in_=gate_d.rearrange("(k p) c -> k p c", p=128)), "d_conv")
    convmark["gate"] = r.ctr["d_conv"]
    for e_ in range(E):
        for h in range(2):
            for part in range(2):
                csrc = part * FI + 3 * h * 128
                cdst = part * 384

                def cf(en, e_=e_, h=h, csrc=csrc, cdst=cdst):
                    src = gup_d[e_ * D:(e_ + 1) * D, csrc:csrc + 384]
                    src = src.rearrange("(k p) c -> k p c", p=128)
                    dst = gup_bf[(e_ * 2 + h) * 128:(e_ * 2 + h + 1) * 128, :]
                    dst = dst.rearrange("p (k c) -> k p c", c=768)
                    return en.dma_start(out=dst[:, :, cdst:cdst + 384], in_=src)
                r.do("g", cf, "d_conv")
        for q in range(4):
            def cf2(en, e_=e_, q=q):
                src = dwn_d[e_ * FI:(e_ + 1) * FI, q * 512:(q + 1) * 512]
                src = src.rearrange("(k p) c -> k p c", p=128)
                dst = dwn_bf[(e_ * 4 + q) * 128:(e_ * 4 + q + 1) * 128, :]
                dst = dst.rearrange("p (k c) -> k p c", c=512)
                return en.dma_start(out=dst, in_=src)
            r.do("g", cf2, "d_conv")
        convmark[("e", e_)] = r.ctr["d_conv"]
    for hg in range(4):
        def cq(en, hg=hg):
            src = qwt_d[:, hg * 512:(hg + 1) * 512].rearrange(
                "(k p) c -> k p c", p=128)
            dst = qwt_bf[hg * 128:(hg + 1) * 128, :].rearrange(
                "p (k c) -> k p c", c=512)
            return en.dma_start(out=dst, in_=src)
        r.do("g", cq, "d_conv")
        def ck(en, hg=hg):
            src = kvwt_d[:, hg * 128:(hg + 1) * 128].rearrange(
                "(k p) c -> k p c", p=128)
            dst = kvwt_bf[hg * 128:(hg + 1) * 128, :].rearrange(
                "p (k c) -> k p c", c=256)
            return en.dma_start(out=dst[:, :, 0:128], in_=src)
        r.do("g", ck, "d_conv")
        def cv(en, hg=hg):
            src = kvwt_d[:, 512 + hg * 128:512 + (hg + 1) * 128].rearrange(
                "(k p) c -> k p c", p=128)
            dst = kvwt_bf[hg * 128:(hg + 1) * 128, :].rearrange(
                "p (k c) -> k p c", c=256)
            return en.dma_start(out=dst[:, :, 128:256], in_=src)
        r.do("g", cv, "d_conv")
    def cc(en):
        src = cwt_d.rearrange("(k p) c -> k p c", p=128)
        dst = cwt_bf.rearrange("p (k c) -> k p c", c=D)
        return en.dma_start(out=dst, in_=src)
    r.do("g", cc, "d_conv")
    convmark["attnw"] = r.ctr["d_conv"]
    r.wait("y", "d_conv", convmark["gate"])
    r.do("y", lambda e: e.dma_start(out=gate_sb[:], in_=gate_bfd[:, :]),
         "d_att")
    gateldmark = r.ctr["d_att"]


    def alloc_ffn_set(sfx):
        a = {}
        order = []
        def al(name, shape, dt):
            gd = nc.sbuf_tensor(name + sfx, list(shape), dt)
            h = gd.__enter__()
            a[name] = h
            order.append(gd)
            return h
        al("xts", (128, 4 * D), BF16)
        al("xn", (128, 4 * D), BF16)
        al("xnt", (128, KB * TS), BF16)
        al("sq", (128, D), F32)
        al("act", (128, G * 6 * TS), BF16)
        al("acc", (128, 16 * TS), F32)
        al("w1", (128, 2 * KB * 768), BF16)
        al("w2", (128, 5 * 6 * 512), BF16)
        al("wb", (128, 2 * TS), BF16)
        al("silu", (128, 2 * TS), BF16)
        al("usb", (128, 2 * TS), BF16)
        al("t1", (128, 2 * TS), BF16)
        al("red", (128, 32), F32)
        al("rlog", (128, 4 * E), F32)
        al("rexp", (128, 4 * E), F32)
        al("rmsk", (128, 4 * E), F32)
        al("r8", (128, 4 * 8), F32)
        al("wr", (128, 4 * E), F32)
        al("wtT", (32, TS), BF16)
        pg = nc.psum_tensor("psg" + sfx, [128, 4 * TS], F32)
        a["psg"] = pg.__enter__(); order.append(pg)
        pd = nc.psum_tensor("psd" + sfx, [128, 4 * 512], F32)
        a["psd"] = pd.__enter__(); order.append(pd)
        a["_order"] = order
        return a

    def free_aset(a):
        for gd in a["_order"][::-1]:
            gd.__exit__(None, None, None)

    st = {"w1slot": 0, "w2slot": 0, "wbslot": 0,
          "w2_rr": 0, "w2_pend": {},
          "w1_fill": [0, 0], "w2_fill": [0] * 5, "wb_fill": [0, 0],
          "w1_cons": [0, 0], "w2_cons": [0] * 5, "wb_cons": [0, 0],
          "xntbuf": 0, "xnt_rd": [0, 0], "accbuf": 0, "accb_rd": [0, 0],
          "wb_rdmark": 0, "xts_free_v": 0, "xts_free_s": 0, "xts_free_d": 0,
          "dn_evac": 0, "pbuf_rd": [0, 0]}

    def w2_load(a, grp, q, ei):
        e_ = grp * G + ei
        slot = st["w2_rr"] % 5
        st["w2_rr"] += 1
        sem = "d_w2a" if slot % 2 == 0 else "d_w2b"
        r.wait("y", "t_m", st["w2_cons"][slot])
        r.do("y", lambda e, e_=e_, q=q, slot=slot: e.dma_start(
            out=a["w2"][:, slot * 6 * 512:(slot + 1) * 6 * 512],
            in_=dwn_bf[(e_ * 4 + q) * 128:(e_ * 4 + q + 1) * 128, :]), sem)
        st["w2_fill"][slot] = r.ctr[sem]
        st["w2_pend"][(grp, q % 4, ei)] = slot

    def ln_chain(a, src_tile, dst_tile, t4):
        red = a["red"]
        c0 = red[:, t4 * 8 + 0:t4 * 8 + 1]
        c1 = red[:, t4 * 8 + 1:t4 * 8 + 2]
        c2 = red[:, t4 * 8 + 2:t4 * 8 + 3]
        c3 = red[:, t4 * 8 + 3:t4 * 8 + 4]
        c4 = red[:, t4 * 8 + 4:t4 * 8 + 5]
        c5 = red[:, t4 * 8 + 5:t4 * 8 + 6]
        c6 = red[:, t4 * 8 + 6:t4 * 8 + 7]
        sq = a["sq"]
        r.wait("s", "v_m")
        r.do("s", lambda e: e.activation(sq[:], src_tile, AF.Square), "sc_m")
        r.wait("v", "sc_m")
        r.do("v", lambda e: e.reduce_sum(c0, src_tile, axis=AX))
        r.do("v", lambda e: e.reduce_sum(c1, sq[:], axis=AX), "v_m")
        r.wait("v", "v_m")
        r.do("v", lambda e: e.tensor_scalar(c2, c0, c_invD, None, op0=ALU.mult))
        r.do("v", lambda e: e.tensor_scalar(c3, c1, c_invD, None, op0=ALU.mult),
             "v_m")
        r.wait("v", "v_m")
        r.do("v", lambda e: e.tensor_tensor(c6, c2, c2, op=ALU.mult), "v_m")
        r.wait("v", "v_m")
        r.do("v", lambda e: e.tensor_tensor(c3, c3, c6, op=ALU.subtract), "v_m")
        r.wait("s", "v_m")
        r.do("s", lambda e: e.activation(c3, c3, AF.Sqrt, bias=c_eps), "sc_m")
        r.wait("v", "sc_m")
        r.do("v", lambda e: e.reciprocal(c4, c3), "v_m")
        r.wait("v", "v_m")
        r.do("v", lambda e: e.scalar_tensor_tensor(
            c5, c2, -1.0, c4, op0=ALU.mult, op1=ALU.mult), "v_m")
        r.wait("s", "v_m")
        r.do("s", lambda e: e.activation(dst_tile, src_tile, AF.Identity,
                                         bias=c5, scale=c4), "sc_m")

    def ffn_ts(a, ts, src_kind, dst_kind):
        xts, xn, xnt = a["xts"], a["xn"], a["xnt"]
        psg, psd = a["psg"], a["psd"]
        rows0 = ts * TS

        ldeng = "g" if src_kind == "x0" else "y"
        r.wait(ldeng, "d_res", st["xts_free_d"])
        r.wait(ldeng, "sc_m", st["xts_free_s"])
        r.wait(ldeng, "v_m", st["xts_free_v"])
        for t4 in range(4):
            rr = rows0 + t4 * TB
            src = x0d if src_kind == "x0" else x2_hbm
            rr2 = rr if src_kind == "x0" else (ts * TS + t4 * TB)
            r.do(ldeng, lambda e, rr2=rr2, t4=t4, src=src: e.dma_start(
                out=xts[:, t4 * D:(t4 + 1) * D], in_=src[rr2:rr2 + TB, :]), "d_x")
        xmark = r.ctr["d_x"]

        r.wait("s", "d_x", xmark)
        r.wait("v", "d_x", xmark)
        r.wait("s", "d_res", st.get("xn_dres", 0))
        for t4 in range(4):
            ln_chain(a, xts[:, t4 * D:(t4 + 1) * D], xn[:, t4 * D:(t4 + 1) * D], t4)
        lnmark = r.ctr["sc_m"]
        buf = st["xntbuf"]
        st["xntbuf"] ^= 1
        r.wait("y", "sc_m", lnmark)
        r.wait("y", "d_xnt", st["xnt_rd"][buf])
        for t4 in range(4):
            r.do("y", lambda e, buf=buf, t4=t4: e.dma_start(
                out=xnt_hbm[buf][t4 * TB:(t4 + 1) * TB, :],
                in_=xn[:, t4 * D:(t4 + 1) * D]), "d_bnc")
        bncmark = r.ctr["d_bnc"]
        r.wait("y", "d_bnc", bncmark)
        r.wait("y", "t_m", st.get("xnt_sb_free", 0))
        for k in range(KB):
            r.do("y", lambda e, buf=buf, k=k: e.dma_start(
                out=xnt[:, k * TS:(k + 1) * TS],
                in_=xnt_hbm[buf][:, k * 128:(k + 1) * 128], transpose=True),
                "d_xnt")
        st["xnt_rd"][buf] = r.ctr["d_xnt"]
        xntmark = r.ctr["d_xnt"]

        r.wait("t", "d_xnt", xntmark)
        r.wait("t", "d_att", gateldmark)
        r.wait("t", "v_m")
        r.wait("t", "sc_m")
        for t4 in range(4):
            for k in range(KB):
                r.do("t", lambda e, t4=t4, k=k: e.matmul(
                    psg[:, 0:E],
                    xnt[:, k * TS + t4 * TB:k * TS + t4 * TB + TB],
                    gate_sb[:, k * E:(k + 1) * E],
                    start=(k == 0), stop=(k == KB - 1)),
                    "t_m" if k == KB - 1 else None)
            r.wait("v", "t_m")
            r.do("v", lambda e, t4=t4: e.tensor_copy(
                a["rlog"][:, t4 * E:(t4 + 1) * E], psg[:, 0:E]), "v_m")
            r.wait("t", "v_m")
        for t4 in range(4):
            r.do("v", lambda e, t4=t4: e.max(
                out=a["r8"][:, t4 * 8:(t4 + 1) * 8],
                in_=a["rlog"][:, t4 * E:(t4 + 1) * E]),
                "v_m" if t4 == 3 else None)
        r.wait("v", "v_m")
        for t4 in range(4):
            r.do("v", lambda e, t4=t4: e.tensor_scalar(
                a["red"][:, t4 * 8 + 7:t4 * 8 + 8],
                a["r8"][:, t4 * 8:t4 * 8 + 1], -1.0, None, op0=ALU.mult),
                "v_m" if t4 == 3 else None)
        r.wait("s", "v_m")
        for t4 in range(4):
            r.do("s", lambda e, t4=t4: e.activation(
                a["rexp"][:, t4 * E:(t4 + 1) * E],
                a["rlog"][:, t4 * E:(t4 + 1) * E], AF.Exp,
                bias=a["red"][:, t4 * 8 + 7:t4 * 8 + 8]),
                "sc_m" if t4 == 3 else None)
        r.wait("v", "sc_m")
        for t4 in range(4):
            r.do("v", lambda e, t4=t4: e.scalar_tensor_tensor(
                a["rmsk"][:, t4 * E:(t4 + 1) * E],
                a["rlog"][:, t4 * E:(t4 + 1) * E],
                a["r8"][:, t4 * 8 + 3:t4 * 8 + 4],
                a["rexp"][:, t4 * E:(t4 + 1) * E],
                op0=ALU.is_ge, op1=ALU.mult),
                "v_m" if t4 == 3 else None)
        r.wait("v", "v_m")
        for t4 in range(4):
            r.do("v", lambda e, t4=t4: e.reduce_sum(
                a["red"][:, t4 * 8 + 6:t4 * 8 + 7],
                a["rmsk"][:, t4 * E:(t4 + 1) * E], axis=AX),
                "v_m" if t4 == 3 else None)
        r.wait("v", "v_m")
        for t4 in range(4):
            r.do("v", lambda e, t4=t4: e.reciprocal(
                a["red"][:, t4 * 8 + 5:t4 * 8 + 6],
                a["red"][:, t4 * 8 + 6:t4 * 8 + 7]),
                "v_m" if t4 == 3 else None)
        r.wait("v", "v_m")
        for t4 in range(4):
            r.do("v", lambda e, t4=t4: e.tensor_scalar(
                a["wr"][:, t4 * E:(t4 + 1) * E],
                a["rmsk"][:, t4 * E:(t4 + 1) * E],
                a["red"][:, t4 * 8 + 5:t4 * 8 + 6], 0.5,
                op0=ALU.mult, op1=ALU.mult),
                "v_m" if t4 == 3 else None)
        wmark = r.ctr["v_m"]
        r.wait("t", "v_m", wmark)
        for t4 in range(4):
            r.do("t", lambda e, t4=t4: e.transpose(
                psg[0:32, 2 * TS:2 * TS + 128],
                a["wr"][:, t4 * E:(t4 + 1) * E],
                iden32_sb[:]), "t_m")
            r.wait("v", "t_m")
            r.do("v", lambda e, t4=t4: e.tensor_copy(
                a["wtT"][:, t4 * TB:(t4 + 1) * TB],
                psg[0:32, 2 * TS:2 * TS + 128]), "v_m")
            r.wait("t", "v_m")
        r.wait("y", "v_m")
        r.wait("y", "d_wb", st["wb_rdmark"])
        r.do("y", lambda e: e.dma_start(out=wt_hbm[:, :], in_=a["wtT"][:]),
             "d_bnc")
        wtmark = r.ctr["d_bnc"]

        pairmark = [r.ctr["sc_m"], r.ctr["sc_m"]]
        r.wait("t", "v_m")
        for grp in range(NG):
            r.wait("y", "d_conv", convmark[("e", grp * G + G - 1)])
            w2_load(a, grp, 0, 0)
            for ei in range(G):
                e_ = grp * G + ei
                cur_slots = []
                for h in range(2):
                    slot = st["w1slot"]
                    st["w1slot"] ^= 1
                    cur_slots.append(slot)
                    sem = "d_w1a" if slot == 0 else "d_w1b"
                    r.wait("y", "t_m", st["w1_cons"][slot])
                    r.do("y", lambda e, e_=e_, h=h, slot=slot: e.dma_start(
                        out=a["w1"][:, slot * KB * 768:(slot + 1) * KB * 768],
                        in_=gup_bf[(e_ * 2 + h) * 128:(e_ * 2 + h + 1) * 128, :]),
                        sem)
                    st["w1_fill"][slot] = r.ctr[sem]
                wslot = st["wbslot"]
                st["wbslot"] ^= 1
                r.wait("s", "d_bnc", wtmark)
                r.wait("s", "v_m", st["wb_cons"][wslot])
                r.do("s", lambda e, e_=e_, wslot=wslot: e.dma_start(
                    out=a["wb"][:, wslot * TS:(wslot + 1) * TS],
                    in_=wt_hbm[e_:e_ + 1, :].to_broadcast([128, TS])), "d_wb")
                st["wb_fill"][wslot] = r.ctr["d_wb"]
                st["wb_rdmark"] = r.ctr["d_wb"]

                for j in range(6):
                    h = j // 3
                    jl = j % 3
                    slot = cur_slots[h]
                    sem = "d_w1a" if slot == 0 else "d_w1b"
                    par = j % 2
                    gofs = par * 2 * TS
                    uofs = par * 2 * TS + TS
                    w1s = a["w1"][:, slot * KB * 768:(slot + 1) * KB * 768]
                    r.wait("t", sem, st["w1_fill"][slot])
                    r.wait("t", "sc_m", pairmark[par])
                    r.wait("t", "d_wb", st["wb_fill"][wslot])
                    for k in range(KB):
                        r.do("t", lambda e, k=k, jl=jl, gofs=gofs, w1s=w1s,
                             xnt=xnt: e.matmul(
                                 psg[:, gofs:gofs + TS],
                                 w1s[:, k * 768 + jl * 128:k * 768 + jl * 128 + 128],
                                 xnt[:, k * TS:(k + 1) * TS],
                                 start=(k == 0), stop=(k == KB - 1)))
                    for k in range(KB):
                        r.do("t", lambda e, k=k, jl=jl, uofs=uofs, w1s=w1s,
                             xnt=xnt: e.matmul(
                                 psg[:, uofs:uofs + TS],
                                 w1s[:, k * 768 + 384 + jl * 128:
                                     k * 768 + 384 + jl * 128 + 128],
                                 xnt[:, k * TS:(k + 1) * TS],
                                 start=(k == 0), stop=(k == KB - 1)),
                             "t_m" if k == KB - 1 else None)
                    st["w1_cons"][slot] = r.ctr["t_m"]
                    r.wait("s", "t_m")
                    r.do("s", lambda e, par=par, gofs=gofs: e.activation(
                        a["silu"][:, par * TS:(par + 1) * TS],
                        psg[:, gofs:gofs + TS], AF.Silu))
                    r.do("s", lambda e, par=par, uofs=uofs: e.activation(
                        a["usb"][:, par * TS:(par + 1) * TS],
                        psg[:, uofs:uofs + TS], AF.Identity), "sc_m")
                    pairmark[par] = r.ctr["sc_m"]
                    r.wait("v", "sc_m")
                    r.wait("v", "d_wb", st["wb_fill"][wslot])
                    r.do("v", lambda e, par=par, wslot=wslot: e.tensor_tensor(
                        a["t1"][:, par * TS:(par + 1) * TS],
                        a["silu"][:, par * TS:(par + 1) * TS],
                        a["wb"][:, wslot * TS:(wslot + 1) * TS],
                        op=ALU.mult), "v_m")
                    r.wait("v", "v_m")
                    r.wait("v", "d_xnt", st.get("act_tail", 0))
                    actdst = a["act"][:, (ei * 6 + j) * TS:(ei * 6 + j + 1) * TS]
                    r.do("v", lambda e, par=par, actdst=actdst: e.tensor_tensor(
                        actdst,
                        a["t1"][:, par * TS:(par + 1) * TS],
                        a["usb"][:, par * TS:(par + 1) * TS], op=ALU.mult),
                        "v_m")
                st["wb_cons"][wslot] = r.ctr["v_m"]
                if grp == NG - 1 and ei == G - 1:
                    st["xnt_sb_free"] = r.ctr["t_m"]
            actmark = r.ctr["v_m"]

            for q in range(4):
                for ei in range(G):
                    if (grp, q, ei) not in st["w2_pend"]:
                        w2_load(a, grp, q, ei)
                qslots = [st["w2_pend"].pop((grp, q, ei)) for ei in range(G)]
                r.wait("t", "v_m", actmark)
                r.wait("t", "v_m", st["dn_evac"])
                for ei in range(G):
                    slot = qslots[ei]
                    sem = "d_w2a" if slot % 2 == 0 else "d_w2b"
                    w2s = a["w2"][:, slot * 6 * 512:(slot + 1) * 6 * 512]
                    r.wait("t", sem, st["w2_fill"][slot])
                    for dl in range(4):
                        for k in range(6):
                            last = (ei == G - 1 and dl == 3 and k == 5)
                            r.do("t", lambda e, ei=ei, dl=dl, k=k, w2s=w2s:
                                 e.matmul(
                                     psd[:, dl * 512:(dl + 1) * 512],
                                     w2s[:, k * 512 + dl * 128:
                                         k * 512 + dl * 128 + 128],
                                     a["act"][:, (ei * 6 + k) * TS:
                                              (ei * 6 + k + 1) * TS],
                                     start=(ei == 0 and k == 0),
                                     stop=(ei == G - 1 and k == 5)),
                                 "t_m" if last else None)
                    if ei == G - 1 and q < 3:
                        w2_load(a, grp, q + 1, 0)
                dmm = r.ctr["t_m"]
                for slot in qslots:
                    st["w2_cons"][slot] = dmm
                r.wait("v", "t_m", dmm)
                if grp == 0:
                    r.do("v", lambda e, q=q: e.tensor_copy(
                        a["acc"][:, q * 2048:(q + 1) * 2048], psd[:, :]), "v_m")
                else:
                    r.do("v", lambda e, q=q: e.tensor_tensor(
                        a["acc"][:, q * 2048:(q + 1) * 2048],
                        a["acc"][:, q * 2048:(q + 1) * 2048],
                        psd[:, :], op=ALU.add), "v_m")
                st["dn_evac"] = r.ctr["v_m"]
        accmark = r.ctr["v_m"]

        acta = a["act"][:, 0:8192]
        ab = st["accbuf"]
        st["accbuf"] ^= 1
        r.wait("v", "v_m", accmark)
        r.wait("v", "t_m")
        for halfd in range(2):
            r.do("v", lambda e, halfd=halfd, acta=acta: e.tensor_copy(
                acta[:, halfd * 4096:(halfd + 1) * 4096],
                a["acc"][:, halfd * 4096:(halfd + 1) * 4096]),
                "v_m" if halfd == 1 else None)
        r.wait("y", "v_m")
        r.wait("y", "d_xnt", st["accb_rd"][ab])
        for dblk in range(16):
            r.do("y", lambda e, ab=ab, dblk=dblk, acta=acta: e.dma_start(
                out=accb_hbm[ab][dblk * TB:(dblk + 1) * TB, :],
                in_=acta[:, dblk * TS:(dblk + 1) * TS]), "d_bnc")
        r.wait("y", "d_bnc")
        for t4 in range(4):
            r.do("y", lambda e, ab=ab, t4=t4, acta=acta: e.dma_start(
                out=acta[:, t4 * D:(t4 + 1) * D],
                in_=accb_hbm[ab][:, t4 * TB:(t4 + 1) * TB], transpose=True),
                "d_xnt")
        st["accb_rd"][ab] = r.ctr["d_xnt"]
        acctmark = r.ctr["d_xnt"]
        st["act_tail"] = acctmark

        r.wait("v", "d_xnt", acctmark)
        if dst_kind == "x1":
            for t4 in range(4):
                r.do("v", lambda e, t4=t4, acta=acta: e.tensor_tensor(
                    xts[:, t4 * D:(t4 + 1) * D],
                    xts[:, t4 * D:(t4 + 1) * D],
                    acta[:, t4 * D:(t4 + 1) * D], op=ALU.add),
                    "v_m" if t4 == 3 else None)
            x1mark = r.ctr["v_m"]
            st["act_tail_v"] = x1mark
            r.wait("y", "v_m", x1mark)
            for t4 in range(4):
                rr = rows0 + t4 * TB
                r.do("y", lambda e, rr=rr, t4=t4: e.dma_start(
                    out=x1_hbm[rr:rr + TB, :],
                    in_=xts[:, t4 * D:(t4 + 1) * D]), "d_res")
            r.wait("s", "d_bnc", bncmark)
            for t4 in range(4):
                ln_chain(a, xts[:, t4 * D:(t4 + 1) * D],
                         xn[:, t4 * D:(t4 + 1) * D], t4)
            r.wait("y", "sc_m")
            for t4 in range(4):
                rr = rows0 + t4 * TB
                r.do("y", lambda e, rr=rr, t4=t4: e.dma_start(
                    out=xn2_hbm[rr:rr + TB, :],
                    in_=xn[:, t4 * D:(t4 + 1) * D]), "d_res")
            st["xts_free_s"] = r.ctr["sc_m"]
            st["xts_free_v"] = x1mark
            st["xts_free_d"] = r.ctr["d_res"]
            st["xn_dres"] = r.ctr["d_res"]
        else:
            for t4 in range(4):
                r.do("v", lambda e, t4=t4, acta=acta: e.tensor_tensor(
                    a["sq"][:], xts[:, t4 * D:(t4 + 1) * D],
                    acta[:, t4 * D:(t4 + 1) * D], op=ALU.add), "v_m")
                r.wait("y", "v_m")
                rr = rows0 + t4 * TB
                r.do("y", lambda e, rr=rr: e.dma_start(
                    out=outd[rr:rr + TB, :], in_=a["sq"][:]), "d_res")
                r.wait("v", "d_res")
            st["act_tail_v"] = r.ctr["v_m"]
            st["xts_free_v"] = r.ctr["v_m"]
            st["xts_free_s"] = 0
            st["xts_free_d"] = r.ctr["d_res"]

    a1 = alloc_ffn_set("A")
    for ts in range(8):
        ffn_ts(a1, ts, "x0", "x1")
    r.barrier()
    if DBG >= 1:
        r.do("y", lambda e: e.dma_start(out=dbg_out["x1"][:, :],
                                        in_=x1_hbm[:, :]), "d_res")
        r.do("y", lambda e: e.dma_start(out=dbg_out["xn2"][:, :],
                                        in_=xn2_hbm[:, :]), "d_res")
        r.barrier()
    free_aset(a1)

    att = {}
    aorder = []
    def aal(name, shape, dt, bname=None):
        gd = nc.sbuf_tensor((bname or name + "_s"), list(shape), dt)
        att[name] = gd.__enter__()
        aorder.append(gd)
        return att[name]

    aal("x2t", (128, 2 * KB * TS), BF16)
    aal("kt", (128, T), BF16)
    aal("vsb", (128, 32 * 128), BF16)
    aal("qt", (128, 4 * T), BF16)
    aal("qwts", (128, KB * 512), BF16)
    aal("kvwts", (128, KB * 256), BF16)
    aal("cos", (64, T), BF16)
    aal("sin", (64, T), BF16)
    aal("ssb", (128, 2 * T), F32)
    aal("pbf", (128, 2 * T), BF16)
    aal("ptile", (128, 8 * 512), BF16)
    aal("avo", (128, 512), BF16)
    aal("rope", (64, 4 * 512), F32)
    aal("tri", (128, 4 * 512), F32)
    aal("dn", (128, 16), F32)
    pga = nc.psum_tensor("psga", [128, 4 * TS], F32)
    att["psg"] = pga.__enter__(); aorder.append(pga)
    pda = nc.psum_tensor("psda", [128, 4 * 512], F32)
    att["psd"] = pda.__enter__(); aorder.append(pda)

    r.do("g", lambda e: e.dma_start(out=att["cos"][:], in_=cosk_d[:]), "d_att")
    r.do("g", lambda e: e.dma_start(out=att["sin"][:], in_=sink_d[:]), "d_att")
    r.do("y", lambda e: e.dma_start(out=att["tri"][:], in_=tri_d[:]), "d_att")
    r.wait("y", "d_conv", convmark["attnw"])

    def rope_apply(psrc, dst, cos, sin, qc, scr0):
        rp = att["rope"]
        sl = lambda i: rp[:, scr0 + i * 512:scr0 + (i + 1) * 512]
        cqs = cos[:, qc * 512:(qc + 1) * 512]
        sqs = sin[:, qc * 512:(qc + 1) * 512]
        r.wait("v", "t_m")
        r.wait("v", "v_m")
        r.do("v", lambda e: e.tensor_tensor(sl(0), psrc[0:64, :], cqs, op=ALU.mult))
        r.do("v", lambda e: e.tensor_tensor(sl(1), psrc[64:128, :], sqs, op=ALU.mult))
        r.do("v", lambda e: e.tensor_tensor(sl(2), psrc[0:64, :], sqs, op=ALU.mult))
        r.do("v", lambda e: e.tensor_tensor(sl(3), psrc[64:128, :], cqs, op=ALU.mult),
             "v_m")
        r.wait("v", "v_m")
        r.do("v", lambda e: e.tensor_tensor(dst[0:64, :], sl(0), sl(1),
                                            op=ALU.subtract))
        r.do("v", lambda e: e.tensor_tensor(dst[64:128, :], sl(2), sl(3),
                                            op=ALU.add), "v_m")

    st["x2t_rd"] = [0, 0]
    for hg in range(4):
        r.do("y", lambda e, hg=hg: e.dma_start(
            out=att["qwts"][:], in_=qwt_bf[hg * 128:(hg + 1) * 128, :]), "d_att")
        r.do("y", lambda e, hg=hg: e.dma_start(
            out=att["kvwts"][:], in_=kvwt_bf[hg * 128:(hg + 1) * 128, :]), "d_att")
        awmark = r.ctr["d_att"]
        st["k_rope"] = [0, 0]
        st["q_rope"] = [0, 0]
        st["v_ev"] = 0
        r.wait("t", "v_m")
        r.wait("t", "sc_m")
        for qc in range(8):
            buf = qc % 2
            x2sem = "d_x2a" if buf == 0 else "d_x2b"
            r.wait("y", "t_m", st["x2t_rd"][buf])
            r.wait("s", "t_m", st["x2t_rd"][buf])
            for k in range(KB):
                r.do("y", lambda e, buf=buf, k=k, qc=qc: e.dma_start(
                         out=att["x2t"][:, (buf * KB + k) * TS:
                                        (buf * KB + k + 1) * TS],
                         in_=xn2_hbm[qc * 512:(qc + 1) * 512,
                                     k * 128:(k + 1) * 128],
                         transpose=True), x2sem)
            xm = r.ctr[x2sem]
            x2ts = att["x2t"][:, buf * KB * TS:(buf + 1) * KB * TS]
            r.wait("t", x2sem, xm)
            r.wait("t", "d_att", awmark)
            kpar = qc % 2
            kps = att["psd"][:, 1024 + kpar * 512:1024 + (kpar + 1) * 512]
            r.wait("t", "v_m", st["k_rope"][kpar])
            for k in range(KB):
                r.do("t", lambda e, k=k, x2ts=x2ts, kps=kps: e.matmul(
                    kps,
                    att["kvwts"][:, k * 256:k * 256 + 128],
                    x2ts[:, k * TS:(k + 1) * TS],
                    start=(k == 0), stop=(k == KB - 1)),
                    "t_m" if k == KB - 1 else None)
            rope_apply(kps,
                       att["kt"][:, qc * 512:(qc + 1) * 512],
                       att["cos"], att["sin"], qc, 0)
            st["k_rope"][kpar] = r.ctr["v_m"]
            r.wait("t", "sc_m", st["v_ev"])
            for t4 in range(4):
                for k in range(KB):
                    r.do("t", lambda e, t4=t4, k=k, x2ts=x2ts: e.matmul(
                        att["psg"][:, 1024 + t4 * 128:1024 + (t4 + 1) * 128],
                        x2ts[:, k * TS + t4 * TB:k * TS + t4 * TB + TB],
                        att["kvwts"][:, k * 256 + 128:k * 256 + 256],
                        start=(k == 0), stop=(k == KB - 1)),
                        "t_m" if k == KB - 1 else None)
            r.wait("s", "t_m")
            for t4 in range(4):
                r.do("s", lambda e, t4=t4, qc=qc: e.activation(
                    att["vsb"][:, (qc * 4 + t4) * 128:(qc * 4 + t4 + 1) * 128],
                    att["psg"][:, 1024 + t4 * 128:1024 + (t4 + 1) * 128],
                    AF.Identity), "sc_m")
            st["v_ev"] = r.ctr["sc_m"]
            for hl in range(4):
                qpar = hl % 2
                qps = att["psg"][:, 512 + qpar * 1024:512 + qpar * 1024 + 512]
                r.wait("t", "v_m", st["q_rope"][qpar])
                for k in range(KB):
                    r.do("t", lambda e, hl=hl, k=k, x2ts=x2ts, qps=qps:
                         e.matmul(
                             qps,
                             att["qwts"][:, k * 512 + hl * 128:
                                         k * 512 + hl * 128 + 128],
                             x2ts[:, k * TS:(k + 1) * TS],
                             start=(k == 0), stop=(k == KB - 1)),
                         "t_m" if k == KB - 1 else None)
                rope_apply(qps,
                           att["qt"][:, hl * T + qc * 512:hl * T + (qc + 1) * 512],
                           att["cos"], att["sin"], qc, 0)
                st["q_rope"][qpar] = r.ctr["v_m"]
            st["x2t_rd"][buf] = r.ctr["t_m"]
        evb = {i: ("sc_m", 0) for i in range(4)}
        st["pslot_free"] = {0: (0, 0), 1: (0, 0)}
        st["av_evac"] = 0
        r.wait("t", "v_m")
        r.wait("t", "sc_m")
        def _av_body(h, qc, pslot, psem, nk2):
            ngrp = (nk2 + 1) // 2
            for g2 in range(ngrp):
                gpar = g2 % 4
                ptsem = ("d_pt0", "d_pt1", "d_x2a", "d_x2b")[gpar]
                n_in = min(2, nk2 - g2 * 2)
                r.wait("y", "t_m", st.get(("pt_grp", gpar), 0))
                for i in range(n_in):
                    kc2 = g2 * 2 + i
                    tsl = gpar * 2 + i
                    r.do("y", lambda e, kc2=kc2, tsl=tsl, pslot=pslot:
                         e.dma_start(
                             out=att["ptile"][:, tsl * 512:(tsl + 1) * 512],
                             in_=p_hbm[pslot][0:512,
                                              kc2 * 128:(kc2 + 1) * 128],
                             transpose=True), ptsem)
                r.wait("t", ptsem)
                if g2 == 0:
                    r.wait("t", "v_m", st["av_evac"])
                for i in range(n_in):
                    kc2 = g2 * 2 + i
                    tsl = gpar * 2 + i
                    last = (kc2 == nk2 - 1)
                    r.do("t", lambda e, kc2=kc2, tsl=tsl, last=last:
                         e.matmul(
                             att["psd"][:, 0:512],
                             att["vsb"][:, kc2 * 128:(kc2 + 1) * 128],
                             att["ptile"][:, tsl * 512:(tsl + 1) * 512],
                             start=(kc2 == 0), stop=last),
                         "t_m" if (i == n_in - 1) else None)
                st[("pt_grp", gpar)] = r.ctr["t_m"]
            avm = r.ctr["t_m"]
            st["pslot_free"][pslot] = (r.ctr["d_pt0"], r.ctr["d_pt1"])
            r.wait("v", "t_m", avm)
            r.wait("v", "d_aw")
            r.do("v", lambda e: e.tensor_copy(att["avo"][:, 0:512],
                                              att["psd"][:, 0:512]), "v_m")
            st["av_evac"] = r.ctr["v_m"]
            r.wait("g", "v_m")
            r.do("g", lambda e, h=h, qc=qc: e.dma_start(
                out=attn_hbm[h * 128:(h + 1) * 128, qc * 512:(qc + 1) * 512],
                in_=att["avo"][:, 0:512]), "d_aw")



        pend_av = [None]

        def run_pend():
            if pend_av[0] is not None:
                pend_av[0]()
                pend_av[0] = None

        for hl in range(4):
            h = hg * 4 + hl
            for qc in range(8):
                pslot = qc % 2
                psem = "d_p0" if pslot == 0 else "d_p1"
                nkc = qc + 1
                kfull = nkc * 512
                fr = st["pslot_free"][pslot]
                r.wait("g", "d_pt0", fr[0])
                r.wait("g", "d_pt1", fr[1])
                for qb4 in range(4):
                    qb = qc * 4 + qb4
                    sslot = qb4 % 2
                    ssb = att["ssb"][:, sslot * T:sslot * T + kfull]
                    pbf = att["pbf"][:, sslot * T:sslot * T + kfull]
                    r.wait("s", "sc_m", st.get(("ssb_rd", sslot), 0))
                    r.wait("v", "sc_m", st.get(("ssb_rd", sslot), 0))
                    for kc in range(nkc):
                        bank = (qb4 % 2) * 2 + kc % 2
                        sem_e, val_e = evb[bank]
                        r.wait("t", sem_e, val_e)
                        r.do("t", lambda e, kc=kc, bank=bank, hl=hl,
                             qb=qb: e.matmul(
                                 att["psg"][:, bank * 512:(bank + 1) * 512],
                                 att["qt"][:, hl * T + qb * 128:
                                           hl * T + qb * 128 + 128],
                                 att["kt"][:, kc * 512:(kc + 1) * 512],
                                 start=True, stop=True), "t_m")
                        if kc < nkc - 1:
                            r.wait("s", "t_m")
                            r.do("s", lambda e, kc=kc, bank=bank, ssb=ssb:
                                 e.activation(
                                     ssb[:, kc * 512:(kc + 1) * 512],
                                     att["psg"][:, bank * 512:(bank + 1) * 512],
                                     AF.Identity), "sc_m")
                            evb[bank] = ("sc_m", r.ctr["sc_m"])
                        else:
                            r.wait("v", "t_m")
                            r.do("v", lambda e, kc=kc, bank=bank, qb=qb,
                                 ssb=ssb: e.tensor_tensor(
                                     ssb[:, kc * 512:(kc + 1) * 512],
                                     att["psg"][:, bank * 512:(bank + 1) * 512],
                                     att["tri"][:, (qb % 4) * 512:
                                                (qb % 4 + 1) * 512],
                                     op=ALU.add), "v_m")
                            evb[bank] = ("v_m", r.ctr["v_m"])
                    r.wait("v", "sc_m")
                    r.wait("v", "v_m")
                    r.do("v", lambda e, ssb=ssb, qb4=qb4: e.reduce_max(
                        att["dn"][:, qb4 * 4:qb4 * 4 + 1],
                        ssb[:, :], axis=AX, negate=True), "v_m")
                    r.wait("v", "v_m")
                    r.do("v", lambda e, qb4=qb4: e.tensor_scalar(
                        att["dn"][:, qb4 * 4 + 1:qb4 * 4 + 2],
                        att["dn"][:, qb4 * 4:qb4 * 4 + 1],
                        float(1.0 / np.sqrt(np.float32(DK))), None,
                        op0=ALU.mult), "v_m")
                    r.wait("s", "v_m")
                    r.wait("s", psem)
                    r.do("s", lambda e, ssb=ssb, pbf=pbf, qb4=qb4:
                         e.activation(
                             pbf[:, :], ssb[:, :], AF.Exp,
                             bias=att["dn"][:, qb4 * 4 + 1:qb4 * 4 + 2],
                             scale=consts_sb[:, 2:3],
                             accum_out=att["dn"][:, qb4 * 4 + 2:qb4 * 4 + 3]),
                         "sc_m")
                    st[("ssb_rd", sslot)] = r.ctr["sc_m"]
                    r.wait("v", "sc_m")
                    r.do("v", lambda e, qb4=qb4: e.reciprocal(
                        att["dn"][:, qb4 * 4:qb4 * 4 + 1],
                        att["dn"][:, qb4 * 4 + 2:qb4 * 4 + 3]), "v_m")
                    r.wait("v", "v_m")
                    r.do("v", lambda e, pbf=pbf, qb4=qb4: e.tensor_scalar(
                        pbf[:, :], pbf[:, :],
                        att["dn"][:, qb4 * 4:qb4 * 4 + 1], None,
                        op0=ALU.mult), "v_m")
                    r.wait("g", "v_m")
                    r.do("g", lambda e, pbf=pbf, kfull=kfull, qb4=qb4,
                         pslot=pslot: e.dma_start(
                             out=p_hbm[pslot][qb4 * 128:(qb4 + 1) * 128,
                                              0:kfull],
                             in_=pbf[:, :]), psem)
                    st[("pbf_wr", sslot)] = r.ctr[psem]
                pwmark = r.ctr[psem]
                run_pend()

                def av_now(h=h, qc=qc, pslot=pslot, psem=psem,
                           pwmark=pwmark):
                    self_qc = qc
                    nk2 = (self_qc + 1) * 4
                    r.wait("y", psem, pwmark)
                    _av_body(h, self_qc, pslot, psem, nk2)
                pend_av[0] = av_now
        run_pend()
        if False:
            for _unused in ():
                nk2 = (qc + 1) * 4
                r.wait("y", psem, pwmark)
    r.barrier()
    if DBG >= 2:
        r.do("y", lambda e: e.dma_start(out=dbg_out["attn"][:, :],
                                        in_=attn_hbm[:, :]), "d_res")
        r.barrier()
    for gd in aorder[::-1]:
        gd.__exit__(None, None, None)

    cset = {}
    corder = []
    def cal(name, shape, dt):
        gd = nc.sbuf_tensor(name + "_c", list(shape), dt)
        cset[name] = gd.__enter__()
        corder.append(gd)
        return cset[name]
    cal("asl", (128, KB * 512), BF16)
    cal("cwt", (128, KB * D), BF16)
    cal("x1t", (128, 4 * D), BF16)
    pgc = nc.psum_tensor("psgc", [128, 4 * TS], F32)
    cset["psg"] = pgc.__enter__(); corder.append(pgc)
    pdc = nc.psum_tensor("psdc", [128, 4 * 512], F32)
    cset["psd"] = pdc.__enter__(); corder.append(pdc)

    def mycopy1(e, lg):
        hoffc = (lg % 2) * 2048
        return e.dma_start(out=attnmy_hbm[:, :],
                           in_=attn_hbm[:, hoffc:hoffc + 2048])
    def mycopy2(e, lg):
        hoffr = (lg % 2) * 2048
        return e.dma_start(out=x1my_hbm[:, :],
                           in_=x1_hbm[hoffr:hoffr + 2048, :])
    def myswitch(e, lg):
        mycopy1(e, lg).then_inc(r.sems["d_att"], 16)
        mycopy2(e, lg).then_inc(r.sems["d_att"], 16)
    r.ev["y"].append(("switch", myswitch, None))
    r.ctr["d_att"] += 32
    r.do("y", lambda e: e.dma_start(out=cset["cwt"][:], in_=cwt_bf[:, :]),
         "d_att")
    cpm = r.ctr["d_att"]

    for tc in range(4):
        r.wait("y", "d_att", cpm)
        r.wait("y", "t_m", st.get("asl_free", 0))
        for k in range(KB):
            r.do("y", lambda e, tc=tc, k=k: e.dma_start(
                out=cset["asl"][:, k * 512:(k + 1) * 512],
                in_=attnmy_hbm[k * 128:(k + 1) * 128,
                               tc * 512:(tc + 1) * 512]), "d_att")
        r.wait("g", "v_m", st.get("x1t_free", 0))
        r.wait("g", "d_att", cpm)
        for t4 in range(4):
            rr = tc * 512 + t4 * TB
            r.do("g", lambda e, rr=rr, t4=t4: e.dma_start(
                out=cset["x1t"][:, t4 * D:(t4 + 1) * D],
                in_=x1my_hbm[rr:rr + TB, :]), "d_x")
        ldm_a = r.ctr["d_att"]
        ldm_x = r.ctr["d_x"]
        r.wait("t", "d_att", ldm_a)
        for t4 in range(4):
            ps = cset["psg"] if t4 % 2 == 0 else cset["psd"]
            r.wait("t", "v_m", st.get(("cp_evac", t4 % 2), 0))
            for k in range(KB):
                for n in range(4):
                    last = (k == KB - 1 and n == 3)
                    r.do("t", lambda e, t4=t4, k=k, n=n, ps=ps: e.matmul(
                        ps[:, n * 512:(n + 1) * 512],
                        cset["asl"][:, k * 512 + t4 * 128:k * 512 + t4 * 128 + 128],
                        cset["cwt"][:, k * D + n * 512:k * D + (n + 1) * 512],
                        start=(k == 0), stop=(k == KB - 1)),
                        "t_m" if last else None)
            r.wait("v", "t_m")
            r.wait("v", "d_x", ldm_x)
            r.do("v", lambda e, t4=t4, ps=ps: e.tensor_tensor(
                cset["x1t"][:, t4 * D:(t4 + 1) * D],
                cset["x1t"][:, t4 * D:(t4 + 1) * D],
                ps[:, 0:D], op=ALU.add), "v_m")
            st[("cp_evac", t4 % 2)] = r.ctr["v_m"]
            r.wait("y", "v_m")
            rr = tc * 512 + t4 * TB
            r.do("y", lambda e, rr=rr, t4=t4: e.dma_start(
                out=x2_hbm[rr:rr + TB, :],
                in_=cset["x1t"][:, t4 * D:(t4 + 1) * D]), "d_res")
        st["asl_free"] = r.ctr["t_m"]
        st["x1t_free"] = r.ctr["v_m"]
        r.wait("g", "d_res")

    r.barrier()
    if DBG >= 2:
        r.do("y", lambda e: e.dma_start(out=dbg_out["x2"][:, :],
                                        in_=x2_hbm[:, :]), "d_res")
        r.barrier()
    for gd in corder[::-1]:
        gd.__exit__(None, None, None)

    a2 = alloc_ffn_set("B")
    st["w1_cons"] = [r.ctr["t_m"], r.ctr["t_m"]]
    st["w2_cons"] = [r.ctr["t_m"]] * 5
    st["w2_pend"] = {}
    st["wb_cons"] = [r.ctr["v_m"], r.ctr["v_m"]]
    st["wb_rdmark"] = r.ctr["d_wb"]
    st["xnt_sb_free"] = r.ctr["t_m"]
    st["dn_evac"] = r.ctr["v_m"]
    st["xts_free_v"] = r.ctr["v_m"]
    st["xts_free_s"] = r.ctr["sc_m"]
    st["xts_free_d"] = r.ctr["d_res"]
    for ts in range(4):
        ffn_ts(a2, ts, "x2", "out")

    r.wait("y", "d_res")
    r.wait("g", "d_res")
    free_aset(a2)

    with nc.Block() as block:
        @block.sync
        def _(e):
            r.emit("y", e)

        @block.gpsimd
        def _(e):
            r.emit("g", e)

        @block.tensor
        def _(e):
            r.emit("t", e)

        @block.vector
        def _(e):
            r.emit("v", e)

        @block.scalar
        def _(e):
            r.emit("s", e)

    return nc



_CACHE = {}


def _host_inputs(L0_attn_x, L0_attn_q_w, L0_attn_kv_w, L0_attn_c_w,
                 L0_ffn_gate, L0_ffn_gate_up_proj, L0_ffn_down_proj):
    bf = ml_dtypes.bfloat16
    x = np.asarray(L0_attn_x, dtype=np.float32)
    qw = np.asarray(L0_attn_q_w, dtype=np.float32)
    kvw = np.asarray(L0_attn_kv_w, dtype=np.float32)
    cw = np.asarray(L0_attn_c_w, dtype=np.float32)
    gate = np.asarray(L0_ffn_gate, dtype=np.float32)
    gup = np.asarray(L0_ffn_gate_up_proj, dtype=np.float32)
    dwn = np.asarray(L0_ffn_down_proj, dtype=np.float32)

    perm = np.zeros(D, dtype=np.int64)
    for h in range(NH):
        b = h * DK
        perm[b:b + 64] = b + np.arange(0, DK, 2)
        perm[b + 64:b + DK] = b + np.arange(1, DK, 2)
    permk = np.zeros(NKV * DK, dtype=np.int64)
    for h in range(NKV):
        b = h * DK
        permk[b:b + 64] = b + np.arange(0, DK, 2)
        permk[b + 64:b + DK] = b + np.arange(1, DK, 2)

    qwt = np.ascontiguousarray(qw.T[:, perm])
    kvwt = kvw.T.copy()
    kvwt[:, :NKV * DK] = kvwt[:, :NKV * DK][:, permk]
    kvwt = np.ascontiguousarray(kvwt)
    cwt = np.ascontiguousarray(cw.T)
    gupf = np.ascontiguousarray(gup.reshape(E * D, 2 * FI))
    dwnf = np.ascontiguousarray(dwn.reshape(E * FI, D))

    pairs = np.arange(DK // 2, dtype=np.float32)
    freqs = (np.float32(1.0) /
             (np.float32(10000.0) ** (np.float32(2.0) * pairs / np.float32(DK))
              ).astype(np.float32)).astype(np.float32)
    pos = np.arange(T, dtype=np.float32)
    ang = (pos[:, None] * freqs[None, :]).astype(np.float32)
    cos = np.cos(ang).astype(np.float32)
    sin = np.sin(ang).astype(np.float32)
    cosq = np.ascontiguousarray(cos.T)
    sinq = np.ascontiguousarray(sin.T)
    cosk = cosq
    sink = sinq

    tri = np.zeros((128, 4 * 512), dtype=np.float32)
    for v in range(4):
        blkcol = np.arange(512) // 128
        for j in range(512):
            if blkcol[j] > v:
                tri[:, v * 512 + j] = -30000.0
            elif blkcol[j] == v:
                jj = j - v * 128
                tri[:, v * 512 + j] = np.where(np.arange(128) >= jj, 0.0,
                                               -30000.0)
    consts = np.zeros((128, 8), dtype=np.float32)
    consts[:, 0] = INVD
    consts[:, 1] = EPS
    consts[:, 2] = 1.0 / np.sqrt(np.float32(DK))
    iden32 = np.eye(128, dtype=np.float32)

    shared = dict(qwt=qwt, kvwt=kvwt, cwt=cwt, gatew=gate, gup=gupf, dwn=dwnf,
                  cosq=cosq, sinq=sinq, cosk=cosk, sink=sink, tri=tri,
                  consts=consts, iden32=iden32)
    in_maps = []
    for c in range(8):
        m = dict(shared)
        m["x0"] = np.ascontiguousarray(x[c // 2])
        in_maps.append(m)
    return in_maps


def kernel(L0_attn_x, L0_attn_q_w, L0_attn_kv_w, L0_attn_c_w,
           L0_ffn_gate, L0_ffn_gate_up_proj, L0_ffn_down_proj):
    if "nc" not in _CACHE:
        _CACHE["nc"] = build_program()
    nc = _CACHE["nc"]
    in_maps = _host_inputs(L0_attn_x, L0_attn_q_w, L0_attn_kv_w, L0_attn_c_w,
                           L0_ffn_gate, L0_ffn_gate_up_proj, L0_ffn_down_proj)
    res = run_bass_kernel_spmd(nc, in_maps, core_ids=list(range(8)))
    out = np.zeros((4, T, D), dtype=np.float32)
    for c in range(8):
        seq, half = c // 2, c % 2
        out[seq, half * 2048:(half + 1) * 2048, :] = res.results[c]["out"]
    return out


# version 21
# speedup vs baseline: 1.1508x; 1.1508x over previous
import numpy as np
import ml_dtypes

import concourse.bass as bass
import concourse.mybir as mybir
from concourse.bass_utils import run_bass_kernel_spmd

F32 = mybir.dt.float32
BF16 = mybir.dt.bfloat16
AF = mybir.ActivationFunctionType
ALU = mybir.AluOpType
AX = mybir.AxisListType.X

D = 2048
T = 4096
NH = 16
NKV = 4
DK = 128
E = 32
FI = 768
TS = 512
TB = 128
KB = D // 128
G = 4
NG = E // G
INVD = 1.0 / D
EPS = 1e-5

DBG = 0


class Rec:
    ENGS = ("y", "g", "t", "v", "s")

    def __init__(self, nc):
        self.nc = nc
        self.ev = {e: [] for e in self.ENGS}
        self.ctr = {}
        self.sems = {}

    def mksem(self, *names):
        for n in names:
            self.sems[n] = self.nc.semaphore(n).__enter__()
            self.ctr[n] = 0

    def do(self, eng, fn, inc=None, n=None):
        tgt = None
        if inc is not None:
            if n is None:
                n = 16 if inc.startswith("d_") else 1
            self.ctr[inc] += n
            tgt = (self.sems[inc], n)
        self.ev[eng].append(("op", fn, tgt))
        return self.ctr[inc] if inc else None

    def wait(self, eng, sem, val=None):
        v = self.ctr[sem] if val is None else val
        if v > 0:
            self.ev[eng].append(("wait", self.sems[sem], v))

    def switch(self, eng, fn):
        self.ev[eng].append(("switch", fn, None))

    def barrier(self):
        snap = dict(self.ctr)
        for e in self.ENGS:
            for s, v in snap.items():
                if v > 0:
                    self.ev[e].append(("wait", self.sems[s], v))

    def emit(self, eng_name, e):
        for kind, a, b in self.ev[eng_name]:
            if kind == "op":
                inst = a(e)
                if b is not None:
                    inst.then_inc(b[0], b[1])
            elif kind == "wait":
                e.wait_ge(a, b)
            else:
                for arm in self.nc.switch_core_id([e], n=128):
                    a(e, arm.logical % 8)


def build_program():
    nc = bass.Bass("TRN2", target_bir_lowering=False)
    r = Rec(nc)

    def ein(name, shape, dt):
        return nc.dram_tensor(name, list(shape), dt, kind="ExternalInput").ap()

    def din(name, shape, dt):
        return nc.dram_tensor(name, list(shape), dt, kind="Internal").ap()

    x0d = ein("x0", (T, D), F32)
    qwt_d = ein("qwt", (D, D), F32)
    kvwt_d = ein("kvwt", (D, 2 * NKV * DK), F32)
    cwt_d = ein("cwt", (D, D), F32)
    gate_d = ein("gatew", (D, E), F32)
    gup_d = ein("gup", (E * D, 2 * FI), F32)
    dwn_d = ein("dwn", (E * FI, D), F32)
    cosq_d = ein("cosq", (64, T), F32)
    sinq_d = ein("sinq", (64, T), F32)
    cosk_d = ein("cosk", (64, T), F32)
    sink_d = ein("sink", (64, T), F32)
    tri_d = ein("tri", (128, 4 * 512), F32)
    consts_d = ein("consts", (128, 8), F32)
    iden32_d = ein("iden32", (128, 128), F32)

    outd = nc.dram_tensor("out", [4 * TS, D], F32, kind="ExternalOutput").ap()
    dbg_out = {}
    if DBG >= 1:
        dbg_out["x1"] = nc.dram_tensor("dbg_x1", [T, D], BF16,
                                       kind="ExternalOutput").ap()
        dbg_out["xn2"] = nc.dram_tensor("dbg_xn2", [T, D], BF16,
                                        kind="ExternalOutput").ap()
    if DBG >= 2:
        dbg_out["attn"] = nc.dram_tensor("dbg_attn", [D, T], BF16,
                                         kind="ExternalOutput").ap()
        dbg_out["x2"] = nc.dram_tensor("dbg_x2", [4 * TS, D], BF16,
                                       kind="ExternalOutput").ap()

    gup_bf = din("gup_bf", (E * 2 * 128, KB * 768), BF16)
    dwn_bf = din("dwn_bf", (E * 4 * 128, 6 * 512), BF16)
    qwt_bf = din("qwt_bf", (4 * 128, KB * 512), BF16)
    kvwt_bf = din("kvwt_bf", (4 * 128, KB * 256), BF16)
    cwt_bf = din("cwt_bf", (128, KB * D), BF16)
    gate_bfd = din("gate_bfd", (128, KB * E), BF16)
    x1_hbm = din("x1_hbm", (T, D), BF16)
    xn2_hbm = din("xn2_hbm", (T, D), BF16)
    x2_hbm = din("x2_hbm", (4 * TS, D), BF16)
    xnt_hbm = [din(f"xnt_hbm{i}", (TS, D), BF16) for i in range(2)]
    accb_hbm = [din(f"accb_hbm{i}", (D, TS), BF16) for i in range(2)]
    wt_hbm = din("wt_hbm", (E, TS), BF16)
    p_hbm = [din(f"p_hbm{i}", (TS, T), BF16) for i in range(3)]
    attn_hbm = din("attn_hbm", (D, T), BF16)
    attnmy_hbm = din("attnmy_hbm", (D, 4 * TS), BF16)
    x1my_hbm = din("x1my_hbm", (4 * TS, D), BF16)

    r.mksem("d_conv", "d_x", "d_bnc", "d_xnt", "d_w1a", "d_w1b",
            "d_w2a", "d_w2b", "d_wb", "d_res", "d_p0", "d_p1",
            "d_pt0", "d_pt1", "d_att", "d_aw", "v_m", "sc_m", "t_m",
            "d_x2a", "d_x2b", "d_p2")

    consts_sb = nc.alloc_sbuf_tensor("consts_sb", [128, 8], F32)
    gate_sb = nc.alloc_sbuf_tensor("gate_sb2", [128, KB * E], BF16)
    iden32_sb = nc.alloc_sbuf_tensor("iden32_sb", [128, 128], F32)
    c_invD = consts_sb[:, 0:1]
    c_eps = consts_sb[:, 1:2]

    r.do("y", lambda e: e.dma_start(out=consts_sb[:], in_=consts_d[:]), "d_att")
    r.do("y", lambda e: e.dma_start(out=iden32_sb[:], in_=iden32_d[:]), "d_att")

    convmark = {}
    r.do("g", lambda e: e.dma_start(
        out=gate_bfd.rearrange("p (k c) -> k p c", c=E),
        in_=gate_d.rearrange("(k p) c -> k p c", p=128)), "d_conv")
    convmark["gate"] = r.ctr["d_conv"]
    for e_ in range(E):
        for h in range(2):
            for part in range(2):
                csrc = part * FI + 3 * h * 128
                cdst = part * 384

                def cf(en, e_=e_, h=h, csrc=csrc, cdst=cdst):
                    src = gup_d[e_ * D:(e_ + 1) * D, csrc:csrc + 384]
                    src = src.rearrange("(k p) c -> k p c", p=128)
                    dst = gup_bf[(e_ * 2 + h) * 128:(e_ * 2 + h + 1) * 128, :]
                    dst = dst.rearrange("p (k c) -> k p c", c=768)
                    return en.dma_start(out=dst[:, :, cdst:cdst + 384], in_=src)
                r.do("g", cf, "d_conv")
        for q in range(4):
            def cf2(en, e_=e_, q=q):
                src = dwn_d[e_ * FI:(e_ + 1) * FI, q * 512:(q + 1) * 512]
                src = src.rearrange("(k p) c -> k p c", p=128)
                dst = dwn_bf[(e_ * 4 + q) * 128:(e_ * 4 + q + 1) * 128, :]
                dst = dst.rearrange("p (k c) -> k p c", c=512)
                return en.dma_start(out=dst, in_=src)
            r.do("g", cf2, "d_conv")
        convmark[("e", e_)] = r.ctr["d_conv"]
    for hg in range(4):
        def cq(en, hg=hg):
            src = qwt_d[:, hg * 512:(hg + 1) * 512].rearrange(
                "(k p) c -> k p c", p=128)
            dst = qwt_bf[hg * 128:(hg + 1) * 128, :].rearrange(
                "p (k c) -> k p c", c=512)
            return en.dma_start(out=dst, in_=src)
        r.do("g", cq, "d_conv")
        def ck(en, hg=hg):
            src = kvwt_d[:, hg * 128:(hg + 1) * 128].rearrange(
                "(k p) c -> k p c", p=128)
            dst = kvwt_bf[hg * 128:(hg + 1) * 128, :].rearrange(
                "p (k c) -> k p c", c=256)
            return en.dma_start(out=dst[:, :, 0:128], in_=src)
        r.do("g", ck, "d_conv")
        def cv(en, hg=hg):
            src = kvwt_d[:, 512 + hg * 128:512 + (hg + 1) * 128].rearrange(
                "(k p) c -> k p c", p=128)
            dst = kvwt_bf[hg * 128:(hg + 1) * 128, :].rearrange(
                "p (k c) -> k p c", c=256)
            return en.dma_start(out=dst[:, :, 128:256], in_=src)
        r.do("g", cv, "d_conv")
    def cc(en):
        src = cwt_d.rearrange("(k p) c -> k p c", p=128)
        dst = cwt_bf.rearrange("p (k c) -> k p c", c=D)
        return en.dma_start(out=dst, in_=src)
    r.do("g", cc, "d_conv")
    convmark["attnw"] = r.ctr["d_conv"]
    r.wait("y", "d_conv", convmark["gate"])
    r.do("y", lambda e: e.dma_start(out=gate_sb[:], in_=gate_bfd[:, :]),
         "d_att")
    gateldmark = r.ctr["d_att"]


    def alloc_ffn_set(sfx):
        a = {}
        order = []
        def al(name, shape, dt):
            gd = nc.sbuf_tensor(name + sfx, list(shape), dt)
            h = gd.__enter__()
            a[name] = h
            order.append(gd)
            return h
        al("xts", (128, 4 * D), BF16)
        al("xn", (128, 4 * D), BF16)
        al("xnt", (128, KB * TS), BF16)
        al("sq", (128, D), F32)
        al("act", (128, G * 6 * TS), BF16)
        al("acc", (128, 16 * TS), F32)
        al("w1", (128, 2 * KB * 768), BF16)
        al("w2", (128, 5 * 6 * 512), BF16)
        al("wb", (128, 2 * TS), BF16)
        al("silu", (128, 2 * TS), BF16)
        al("usb", (128, 2 * TS), BF16)
        al("t1", (128, 2 * TS), BF16)
        al("red", (128, 32), F32)
        al("rlog", (128, 4 * E), F32)
        al("rexp", (128, 4 * E), F32)
        al("rmsk", (128, 4 * E), F32)
        al("r8", (128, 4 * 8), F32)
        al("wr", (128, 4 * E), F32)
        al("wtT", (32, TS), BF16)
        pg = nc.psum_tensor("psg" + sfx, [128, 4 * TS], F32)
        a["psg"] = pg.__enter__(); order.append(pg)
        pd = nc.psum_tensor("psd" + sfx, [128, 4 * 512], F32)
        a["psd"] = pd.__enter__(); order.append(pd)
        a["_order"] = order
        return a

    def free_aset(a):
        for gd in a["_order"][::-1]:
            gd.__exit__(None, None, None)

    st = {"w1slot": 0, "w2slot": 0, "wbslot": 0,
          "w2_rr": 0, "w2_pend": {},
          "w1_fill": [0, 0], "w2_fill": [0] * 5, "wb_fill": [0, 0],
          "w1_cons": [0, 0], "w2_cons": [0] * 5, "wb_cons": [0, 0],
          "xntbuf": 0, "xnt_rd": [0, 0], "accbuf": 0, "accb_rd": [0, 0],
          "wb_rdmark": 0, "xts_free_v": 0, "xts_free_s": 0, "xts_free_d": 0,
          "dn_evac": 0, "pbuf_rd": [0, 0]}

    def w2_load(a, grp, q, ei):
        e_ = grp * G + ei
        slot = st["w2_rr"] % 5
        st["w2_rr"] += 1
        sem = "d_w2a" if slot % 2 == 0 else "d_w2b"
        r.wait("y", "t_m", st["w2_cons"][slot])
        r.do("y", lambda e, e_=e_, q=q, slot=slot: e.dma_start(
            out=a["w2"][:, slot * 6 * 512:(slot + 1) * 6 * 512],
            in_=dwn_bf[(e_ * 4 + q) * 128:(e_ * 4 + q + 1) * 128, :]), sem)
        st["w2_fill"][slot] = r.ctr[sem]
        st["w2_pend"][(grp, q % 4, ei)] = slot

    def ln_chain(a, src_tile, dst_tile, t4):
        red = a["red"]
        c0 = red[:, t4 * 8 + 0:t4 * 8 + 1]
        c1 = red[:, t4 * 8 + 1:t4 * 8 + 2]
        c2 = red[:, t4 * 8 + 2:t4 * 8 + 3]
        c3 = red[:, t4 * 8 + 3:t4 * 8 + 4]
        c4 = red[:, t4 * 8 + 4:t4 * 8 + 5]
        c5 = red[:, t4 * 8 + 5:t4 * 8 + 6]
        c6 = red[:, t4 * 8 + 6:t4 * 8 + 7]
        sq = a["sq"]
        r.wait("s", "v_m")
        r.do("s", lambda e: e.activation(sq[:], src_tile, AF.Square), "sc_m")
        r.wait("v", "sc_m")
        r.do("v", lambda e: e.reduce_sum(c0, src_tile, axis=AX))
        r.do("v", lambda e: e.reduce_sum(c1, sq[:], axis=AX), "v_m")
        r.wait("v", "v_m")
        r.do("v", lambda e: e.tensor_scalar(c2, c0, c_invD, None, op0=ALU.mult))
        r.do("v", lambda e: e.tensor_scalar(c3, c1, c_invD, None, op0=ALU.mult),
             "v_m")
        r.wait("v", "v_m")
        r.do("v", lambda e: e.tensor_tensor(c6, c2, c2, op=ALU.mult), "v_m")
        r.wait("v", "v_m")
        r.do("v", lambda e: e.tensor_tensor(c3, c3, c6, op=ALU.subtract), "v_m")
        r.wait("s", "v_m")
        r.do("s", lambda e: e.activation(c3, c3, AF.Sqrt, bias=c_eps), "sc_m")
        r.wait("v", "sc_m")
        r.do("v", lambda e: e.reciprocal(c4, c3), "v_m")
        r.wait("v", "v_m")
        r.do("v", lambda e: e.scalar_tensor_tensor(
            c5, c2, -1.0, c4, op0=ALU.mult, op1=ALU.mult), "v_m")
        r.wait("s", "v_m")
        r.do("s", lambda e: e.activation(dst_tile, src_tile, AF.Identity,
                                         bias=c5, scale=c4), "sc_m")

    def ffn_ts(a, ts, src_kind, dst_kind):
        xts, xn, xnt = a["xts"], a["xn"], a["xnt"]
        psg, psd = a["psg"], a["psd"]
        rows0 = ts * TS

        ldeng = "g" if src_kind == "x0" else "y"
        r.wait(ldeng, "d_res", st["xts_free_d"])
        r.wait(ldeng, "sc_m", st["xts_free_s"])
        r.wait(ldeng, "v_m", st["xts_free_v"])
        for t4 in range(4):
            rr = rows0 + t4 * TB
            src = x0d if src_kind == "x0" else x2_hbm
            rr2 = rr if src_kind == "x0" else (ts * TS + t4 * TB)
            r.do(ldeng, lambda e, rr2=rr2, t4=t4, src=src: e.dma_start(
                out=xts[:, t4 * D:(t4 + 1) * D], in_=src[rr2:rr2 + TB, :]), "d_x")
        xmark = r.ctr["d_x"]

        r.wait("s", "d_x", xmark)
        r.wait("v", "d_x", xmark)
        r.wait("s", "d_res", st.get("xn_dres", 0))
        for t4 in range(4):
            ln_chain(a, xts[:, t4 * D:(t4 + 1) * D], xn[:, t4 * D:(t4 + 1) * D], t4)
        lnmark = r.ctr["sc_m"]
        buf = st["xntbuf"]
        st["xntbuf"] ^= 1
        r.wait("y", "sc_m", lnmark)
        r.wait("y", "d_xnt", st["xnt_rd"][buf])
        for t4 in range(4):
            r.do("y", lambda e, buf=buf, t4=t4: e.dma_start(
                out=xnt_hbm[buf][t4 * TB:(t4 + 1) * TB, :],
                in_=xn[:, t4 * D:(t4 + 1) * D]), "d_bnc")
        bncmark = r.ctr["d_bnc"]
        r.wait("y", "d_bnc", bncmark)
        r.wait("y", "t_m", st.get("xnt_sb_free", 0))
        for k in range(KB):
            r.do("y", lambda e, buf=buf, k=k: e.dma_start(
                out=xnt[:, k * TS:(k + 1) * TS],
                in_=xnt_hbm[buf][:, k * 128:(k + 1) * 128], transpose=True),
                "d_xnt")
        st["xnt_rd"][buf] = r.ctr["d_xnt"]
        xntmark = r.ctr["d_xnt"]

        r.wait("t", "d_xnt", xntmark)
        r.wait("t", "d_att", gateldmark)
        r.wait("t", "v_m")
        r.wait("t", "sc_m")
        for t4 in range(4):
            for k in range(KB):
                r.do("t", lambda e, t4=t4, k=k: e.matmul(
                    psg[:, 0:E],
                    xnt[:, k * TS + t4 * TB:k * TS + t4 * TB + TB],
                    gate_sb[:, k * E:(k + 1) * E],
                    start=(k == 0), stop=(k == KB - 1)),
                    "t_m" if k == KB - 1 else None)
            r.wait("v", "t_m")
            r.do("v", lambda e, t4=t4: e.tensor_copy(
                a["rlog"][:, t4 * E:(t4 + 1) * E], psg[:, 0:E]), "v_m")
            r.wait("t", "v_m")
        for t4 in range(4):
            r.do("v", lambda e, t4=t4: e.max(
                out=a["r8"][:, t4 * 8:(t4 + 1) * 8],
                in_=a["rlog"][:, t4 * E:(t4 + 1) * E]),
                "v_m" if t4 == 3 else None)
        r.wait("v", "v_m")
        for t4 in range(4):
            r.do("v", lambda e, t4=t4: e.tensor_scalar(
                a["red"][:, t4 * 8 + 7:t4 * 8 + 8],
                a["r8"][:, t4 * 8:t4 * 8 + 1], -1.0, None, op0=ALU.mult),
                "v_m" if t4 == 3 else None)
        r.wait("s", "v_m")
        for t4 in range(4):
            r.do("s", lambda e, t4=t4: e.activation(
                a["rexp"][:, t4 * E:(t4 + 1) * E],
                a["rlog"][:, t4 * E:(t4 + 1) * E], AF.Exp,
                bias=a["red"][:, t4 * 8 + 7:t4 * 8 + 8]),
                "sc_m" if t4 == 3 else None)
        r.wait("v", "sc_m")
        for t4 in range(4):
            r.do("v", lambda e, t4=t4: e.scalar_tensor_tensor(
                a["rmsk"][:, t4 * E:(t4 + 1) * E],
                a["rlog"][:, t4 * E:(t4 + 1) * E],
                a["r8"][:, t4 * 8 + 3:t4 * 8 + 4],
                a["rexp"][:, t4 * E:(t4 + 1) * E],
                op0=ALU.is_ge, op1=ALU.mult),
                "v_m" if t4 == 3 else None)
        r.wait("v", "v_m")
        for t4 in range(4):
            r.do("v", lambda e, t4=t4: e.reduce_sum(
                a["red"][:, t4 * 8 + 6:t4 * 8 + 7],
                a["rmsk"][:, t4 * E:(t4 + 1) * E], axis=AX),
                "v_m" if t4 == 3 else None)
        r.wait("v", "v_m")
        for t4 in range(4):
            r.do("v", lambda e, t4=t4: e.reciprocal(
                a["red"][:, t4 * 8 + 5:t4 * 8 + 6],
                a["red"][:, t4 * 8 + 6:t4 * 8 + 7]),
                "v_m" if t4 == 3 else None)
        r.wait("v", "v_m")
        for t4 in range(4):
            r.do("v", lambda e, t4=t4: e.tensor_scalar(
                a["wr"][:, t4 * E:(t4 + 1) * E],
                a["rmsk"][:, t4 * E:(t4 + 1) * E],
                a["red"][:, t4 * 8 + 5:t4 * 8 + 6], 0.5,
                op0=ALU.mult, op1=ALU.mult),
                "v_m" if t4 == 3 else None)
        wmark = r.ctr["v_m"]
        r.wait("t", "v_m", wmark)
        for t4 in range(4):
            r.do("t", lambda e, t4=t4: e.transpose(
                psg[0:32, 2 * TS:2 * TS + 128],
                a["wr"][:, t4 * E:(t4 + 1) * E],
                iden32_sb[:]), "t_m")
            r.wait("v", "t_m")
            r.do("v", lambda e, t4=t4: e.tensor_copy(
                a["wtT"][:, t4 * TB:(t4 + 1) * TB],
                psg[0:32, 2 * TS:2 * TS + 128]), "v_m")
            r.wait("t", "v_m")
        r.wait("y", "v_m")
        r.wait("y", "d_wb", st["wb_rdmark"])
        r.do("y", lambda e: e.dma_start(out=wt_hbm[:, :], in_=a["wtT"][:]),
             "d_bnc")
        wtmark = r.ctr["d_bnc"]

        pairmark = [r.ctr["sc_m"], r.ctr["sc_m"]]
        r.wait("t", "v_m")
        for grp in range(NG):
            r.wait("y", "d_conv", convmark[("e", grp * G + G - 1)])
            w2_load(a, grp, 0, 0)
            for ei in range(G):
                e_ = grp * G + ei
                cur_slots = []
                for h in range(2):
                    slot = st["w1slot"]
                    st["w1slot"] ^= 1
                    cur_slots.append(slot)
                    sem = "d_w1a" if slot == 0 else "d_w1b"
                    r.wait("y", "t_m", st["w1_cons"][slot])
                    r.do("y", lambda e, e_=e_, h=h, slot=slot: e.dma_start(
                        out=a["w1"][:, slot * KB * 768:(slot + 1) * KB * 768],
                        in_=gup_bf[(e_ * 2 + h) * 128:(e_ * 2 + h + 1) * 128, :]),
                        sem)
                    st["w1_fill"][slot] = r.ctr[sem]
                wslot = st["wbslot"]
                st["wbslot"] ^= 1
                r.wait("y", "d_bnc", wtmark)
                r.wait("y", "v_m", st["wb_cons"][wslot])
                r.do("y", lambda e, e_=e_, wslot=wslot: e.dma_start(
                    out=a["wb"][:, wslot * TS:(wslot + 1) * TS],
                    in_=wt_hbm[e_:e_ + 1, :].to_broadcast([128, TS])), "d_wb")
                st["wb_fill"][wslot] = r.ctr["d_wb"]
                st["wb_rdmark"] = r.ctr["d_wb"]

                for j in range(6):
                    h = j // 3
                    jl = j % 3
                    slot = cur_slots[h]
                    sem = "d_w1a" if slot == 0 else "d_w1b"
                    par = j % 2
                    gofs = par * 2 * TS
                    uofs = par * 2 * TS + TS
                    w1s = a["w1"][:, slot * KB * 768:(slot + 1) * KB * 768]
                    r.wait("t", sem, st["w1_fill"][slot])
                    r.wait("t", "sc_m", pairmark[par])
                    r.wait("t", "d_wb", st["wb_fill"][wslot])
                    for k in range(KB):
                        r.do("t", lambda e, k=k, jl=jl, gofs=gofs, w1s=w1s,
                             xnt=xnt: e.matmul(
                                 psg[:, gofs:gofs + TS],
                                 w1s[:, k * 768 + jl * 128:k * 768 + jl * 128 + 128],
                                 xnt[:, k * TS:(k + 1) * TS],
                                 start=(k == 0), stop=(k == KB - 1)))
                    for k in range(KB):
                        r.do("t", lambda e, k=k, jl=jl, uofs=uofs, w1s=w1s,
                             xnt=xnt: e.matmul(
                                 psg[:, uofs:uofs + TS],
                                 w1s[:, k * 768 + 384 + jl * 128:
                                     k * 768 + 384 + jl * 128 + 128],
                                 xnt[:, k * TS:(k + 1) * TS],
                                 start=(k == 0), stop=(k == KB - 1)),
                             "t_m" if k == KB - 1 else None)
                    st["w1_cons"][slot] = r.ctr["t_m"]
                    r.wait("s", "t_m")
                    r.do("s", lambda e, par=par, gofs=gofs: e.activation(
                        a["silu"][:, par * TS:(par + 1) * TS],
                        psg[:, gofs:gofs + TS], AF.Silu))
                    r.do("s", lambda e, par=par, uofs=uofs: e.activation(
                        a["usb"][:, par * TS:(par + 1) * TS],
                        psg[:, uofs:uofs + TS], AF.Identity), "sc_m")
                    pairmark[par] = r.ctr["sc_m"]
                    r.wait("v", "sc_m")
                    r.wait("v", "d_wb", st["wb_fill"][wslot])
                    r.do("v", lambda e, par=par, wslot=wslot: e.tensor_tensor(
                        a["t1"][:, par * TS:(par + 1) * TS],
                        a["silu"][:, par * TS:(par + 1) * TS],
                        a["wb"][:, wslot * TS:(wslot + 1) * TS],
                        op=ALU.mult), "v_m")
                    r.wait("v", "v_m")
                    r.wait("v", "d_xnt", st.get("act_tail", 0))
                    actdst = a["act"][:, (ei * 6 + j) * TS:(ei * 6 + j + 1) * TS]
                    r.do("v", lambda e, par=par, actdst=actdst: e.tensor_tensor(
                        actdst,
                        a["t1"][:, par * TS:(par + 1) * TS],
                        a["usb"][:, par * TS:(par + 1) * TS], op=ALU.mult),
                        "v_m")
                st["wb_cons"][wslot] = r.ctr["v_m"]
                if grp == NG - 1 and ei == G - 1:
                    st["xnt_sb_free"] = r.ctr["t_m"]
            actmark = r.ctr["v_m"]

            for q in range(4):
                for ei in range(G):
                    if (grp, q, ei) not in st["w2_pend"]:
                        w2_load(a, grp, q, ei)
                qslots = [st["w2_pend"].pop((grp, q, ei)) for ei in range(G)]
                r.wait("t", "v_m", actmark)
                r.wait("t", "v_m", st["dn_evac"])
                for ei in range(G):
                    slot = qslots[ei]
                    sem = "d_w2a" if slot % 2 == 0 else "d_w2b"
                    w2s = a["w2"][:, slot * 6 * 512:(slot + 1) * 6 * 512]
                    r.wait("t", sem, st["w2_fill"][slot])
                    for dl in range(4):
                        for k in range(6):
                            last = (ei == G - 1 and dl == 3 and k == 5)
                            r.do("t", lambda e, ei=ei, dl=dl, k=k, w2s=w2s:
                                 e.matmul(
                                     psd[:, dl * 512:(dl + 1) * 512],
                                     w2s[:, k * 512 + dl * 128:
                                         k * 512 + dl * 128 + 128],
                                     a["act"][:, (ei * 6 + k) * TS:
                                              (ei * 6 + k + 1) * TS],
                                     start=(ei == 0 and k == 0),
                                     stop=(ei == G - 1 and k == 5)),
                                 "t_m" if last else None)
                    if ei == G - 1 and q < 3:
                        w2_load(a, grp, q + 1, 0)
                dmm = r.ctr["t_m"]
                for slot in qslots:
                    st["w2_cons"][slot] = dmm
                r.wait("v", "t_m", dmm)
                if grp == 0:
                    r.do("v", lambda e, q=q: e.tensor_copy(
                        a["acc"][:, q * 2048:(q + 1) * 2048], psd[:, :]), "v_m")
                else:
                    r.do("v", lambda e, q=q: e.tensor_tensor(
                        a["acc"][:, q * 2048:(q + 1) * 2048],
                        a["acc"][:, q * 2048:(q + 1) * 2048],
                        psd[:, :], op=ALU.add), "v_m")
                st["dn_evac"] = r.ctr["v_m"]
        accmark = r.ctr["v_m"]

        acta = a["act"][:, 0:8192]
        ab = st["accbuf"]
        st["accbuf"] ^= 1
        r.wait("v", "v_m", accmark)
        r.wait("v", "t_m")
        for halfd in range(2):
            r.do("v", lambda e, halfd=halfd, acta=acta: e.tensor_copy(
                acta[:, halfd * 4096:(halfd + 1) * 4096],
                a["acc"][:, halfd * 4096:(halfd + 1) * 4096]),
                "v_m" if halfd == 1 else None)
        r.wait("y", "v_m")
        r.wait("y", "d_xnt", st["accb_rd"][ab])
        for dblk in range(16):
            r.do("y", lambda e, ab=ab, dblk=dblk, acta=acta: e.dma_start(
                out=accb_hbm[ab][dblk * TB:(dblk + 1) * TB, :],
                in_=acta[:, dblk * TS:(dblk + 1) * TS]), "d_bnc")
        r.wait("y", "d_bnc")
        for t4 in range(4):
            r.do("y", lambda e, ab=ab, t4=t4, acta=acta: e.dma_start(
                out=acta[:, t4 * D:(t4 + 1) * D],
                in_=accb_hbm[ab][:, t4 * TB:(t4 + 1) * TB], transpose=True),
                "d_xnt")
        st["accb_rd"][ab] = r.ctr["d_xnt"]
        acctmark = r.ctr["d_xnt"]
        st["act_tail"] = acctmark

        r.wait("v", "d_xnt", acctmark)
        if dst_kind == "x1":
            for t4 in range(4):
                r.do("v", lambda e, t4=t4, acta=acta: e.tensor_tensor(
                    xts[:, t4 * D:(t4 + 1) * D],
                    xts[:, t4 * D:(t4 + 1) * D],
                    acta[:, t4 * D:(t4 + 1) * D], op=ALU.add),
                    "v_m" if t4 == 3 else None)
            x1mark = r.ctr["v_m"]
            st["act_tail_v"] = x1mark
            r.wait("y", "v_m", x1mark)
            for t4 in range(4):
                rr = rows0 + t4 * TB
                r.do("y", lambda e, rr=rr, t4=t4: e.dma_start(
                    out=x1_hbm[rr:rr + TB, :],
                    in_=xts[:, t4 * D:(t4 + 1) * D]), "d_res")
            r.wait("s", "d_bnc", bncmark)
            for t4 in range(4):
                ln_chain(a, xts[:, t4 * D:(t4 + 1) * D],
                         xn[:, t4 * D:(t4 + 1) * D], t4)
            r.wait("y", "sc_m")
            for t4 in range(4):
                rr = rows0 + t4 * TB
                r.do("y", lambda e, rr=rr, t4=t4: e.dma_start(
                    out=xn2_hbm[rr:rr + TB, :],
                    in_=xn[:, t4 * D:(t4 + 1) * D]), "d_res")
            st["xts_free_s"] = r.ctr["sc_m"]
            st["xts_free_v"] = x1mark
            st["xts_free_d"] = r.ctr["d_res"]
            st["xn_dres"] = r.ctr["d_res"]
        else:
            for t4 in range(4):
                r.do("v", lambda e, t4=t4, acta=acta: e.tensor_tensor(
                    a["sq"][:], xts[:, t4 * D:(t4 + 1) * D],
                    acta[:, t4 * D:(t4 + 1) * D], op=ALU.add), "v_m")
                r.wait("y", "v_m")
                rr = rows0 + t4 * TB
                r.do("y", lambda e, rr=rr: e.dma_start(
                    out=outd[rr:rr + TB, :], in_=a["sq"][:]), "d_res")
                r.wait("v", "d_res")
            st["act_tail_v"] = r.ctr["v_m"]
            st["xts_free_v"] = r.ctr["v_m"]
            st["xts_free_s"] = 0
            st["xts_free_d"] = r.ctr["d_res"]

    a1 = alloc_ffn_set("A")
    for ts in range(8):
        ffn_ts(a1, ts, "x0", "x1")
    r.barrier()
    if DBG >= 1:
        r.do("y", lambda e: e.dma_start(out=dbg_out["x1"][:, :],
                                        in_=x1_hbm[:, :]), "d_res")
        r.do("y", lambda e: e.dma_start(out=dbg_out["xn2"][:, :],
                                        in_=xn2_hbm[:, :]), "d_res")
        r.barrier()
    free_aset(a1)

    att = {}
    aorder = []
    def aal(name, shape, dt, bname=None):
        gd = nc.sbuf_tensor((bname or name + "_s"), list(shape), dt)
        att[name] = gd.__enter__()
        aorder.append(gd)
        return att[name]

    aal("x2t", (128, 2 * KB * TS), BF16)
    aal("kt", (128, T), BF16)
    aal("vsb", (128, 32 * 128), BF16)
    aal("qt", (128, 4 * T), BF16)
    aal("qwts", (128, KB * 512), BF16)
    aal("kvwts", (128, KB * 256), BF16)
    aal("cos", (64, T), BF16)
    aal("sin", (64, T), BF16)
    aal("ssb", (128, 2 * T), F32)
    aal("pbf", (128, 2 * T), BF16)
    aal("ptile", (128, 8 * 512), BF16)
    aal("avo", (128, 512), BF16)
    aal("rope", (64, 4 * 512), F32)
    aal("tri", (128, 4 * 512), F32)
    aal("dn", (128, 16), F32)
    pga = nc.psum_tensor("psga", [128, 4 * TS], F32)
    att["psg"] = pga.__enter__(); aorder.append(pga)
    pda = nc.psum_tensor("psda", [128, 4 * 512], F32)
    att["psd"] = pda.__enter__(); aorder.append(pda)

    r.do("g", lambda e: e.dma_start(out=att["cos"][:], in_=cosk_d[:]), "d_att")
    r.do("g", lambda e: e.dma_start(out=att["sin"][:], in_=sink_d[:]), "d_att")
    r.do("y", lambda e: e.dma_start(out=att["tri"][:], in_=tri_d[:]), "d_att")
    r.wait("y", "d_conv", convmark["attnw"])

    def rope_apply(psrc, dst, cos, sin, qc, scr0):
        rp = att["rope"]
        sl = lambda i: rp[:, scr0 + i * 512:scr0 + (i + 1) * 512]
        cqs = cos[:, qc * 512:(qc + 1) * 512]
        sqs = sin[:, qc * 512:(qc + 1) * 512]
        r.wait("v", "t_m")
        r.wait("v", "v_m")
        r.do("v", lambda e: e.tensor_tensor(sl(0), psrc[0:64, :], cqs, op=ALU.mult))
        r.do("v", lambda e: e.tensor_tensor(sl(1), psrc[64:128, :], sqs, op=ALU.mult))
        r.do("v", lambda e: e.tensor_tensor(sl(2), psrc[0:64, :], sqs, op=ALU.mult))
        r.do("v", lambda e: e.tensor_tensor(sl(3), psrc[64:128, :], cqs, op=ALU.mult),
             "v_m")
        r.wait("v", "v_m")
        r.do("v", lambda e: e.tensor_tensor(dst[0:64, :], sl(0), sl(1),
                                            op=ALU.subtract))
        r.do("v", lambda e: e.tensor_tensor(dst[64:128, :], sl(2), sl(3),
                                            op=ALU.add), "v_m")

    st["x2t_rd"] = [0, 0]
    for hg in range(4):
        r.do("y", lambda e, hg=hg: e.dma_start(
            out=att["qwts"][:], in_=qwt_bf[hg * 128:(hg + 1) * 128, :]), "d_att")
        r.do("y", lambda e, hg=hg: e.dma_start(
            out=att["kvwts"][:], in_=kvwt_bf[hg * 128:(hg + 1) * 128, :]), "d_att")
        awmark = r.ctr["d_att"]
        st["k_rope"] = [0, 0]
        st["q_rope"] = [0, 0]
        st["v_ev"] = 0
        r.wait("t", "v_m")
        r.wait("t", "sc_m")
        for qc in range(8):
            buf = qc % 2
            x2sem = "d_x2a" if buf == 0 else "d_x2b"
            r.wait("y", "t_m", st["x2t_rd"][buf])
            r.wait("s", "t_m", st["x2t_rd"][buf])
            for k in range(KB):
                r.do("y", lambda e, buf=buf, k=k, qc=qc: e.dma_start(
                         out=att["x2t"][:, (buf * KB + k) * TS:
                                        (buf * KB + k + 1) * TS],
                         in_=xn2_hbm[qc * 512:(qc + 1) * 512,
                                     k * 128:(k + 1) * 128],
                         transpose=True), x2sem)
            xm = r.ctr[x2sem]
            x2ts = att["x2t"][:, buf * KB * TS:(buf + 1) * KB * TS]
            r.wait("t", x2sem, xm)
            r.wait("t", "d_att", awmark)
            kpar = qc % 2
            kps = att["psd"][:, 1024 + kpar * 512:1024 + (kpar + 1) * 512]
            r.wait("t", "v_m", st["k_rope"][kpar])
            for k in range(KB):
                r.do("t", lambda e, k=k, x2ts=x2ts, kps=kps: e.matmul(
                    kps,
                    att["kvwts"][:, k * 256:k * 256 + 128],
                    x2ts[:, k * TS:(k + 1) * TS],
                    start=(k == 0), stop=(k == KB - 1)),
                    "t_m" if k == KB - 1 else None)
            rope_apply(kps,
                       att["kt"][:, qc * 512:(qc + 1) * 512],
                       att["cos"], att["sin"], qc, 0)
            st["k_rope"][kpar] = r.ctr["v_m"]
            r.wait("t", "sc_m", st["v_ev"])
            for t4 in range(4):
                for k in range(KB):
                    r.do("t", lambda e, t4=t4, k=k, x2ts=x2ts: e.matmul(
                        att["psg"][:, 1024 + t4 * 128:1024 + (t4 + 1) * 128],
                        x2ts[:, k * TS + t4 * TB:k * TS + t4 * TB + TB],
                        att["kvwts"][:, k * 256 + 128:k * 256 + 256],
                        start=(k == 0), stop=(k == KB - 1)),
                        "t_m" if k == KB - 1 else None)
            r.wait("s", "t_m")
            for t4 in range(4):
                r.do("s", lambda e, t4=t4, qc=qc: e.activation(
                    att["vsb"][:, (qc * 4 + t4) * 128:(qc * 4 + t4 + 1) * 128],
                    att["psg"][:, 1024 + t4 * 128:1024 + (t4 + 1) * 128],
                    AF.Identity), "sc_m")
            st["v_ev"] = r.ctr["sc_m"]
            for hl in range(4):
                qpar = hl % 2
                qps = att["psg"][:, 512 + qpar * 1024:512 + qpar * 1024 + 512]
                r.wait("t", "v_m", st["q_rope"][qpar])
                for k in range(KB):
                    r.do("t", lambda e, hl=hl, k=k, x2ts=x2ts, qps=qps:
                         e.matmul(
                             qps,
                             att["qwts"][:, k * 512 + hl * 128:
                                         k * 512 + hl * 128 + 128],
                             x2ts[:, k * TS:(k + 1) * TS],
                             start=(k == 0), stop=(k == KB - 1)),
                         "t_m" if k == KB - 1 else None)
                rope_apply(qps,
                           att["qt"][:, hl * T + qc * 512:hl * T + (qc + 1) * 512],
                           att["cos"], att["sin"], qc, 0)
                st["q_rope"][qpar] = r.ctr["v_m"]
            st["x2t_rd"][buf] = r.ctr["t_m"]
        evb = {i: ("sc_m", 0) for i in range(4)}
        st["pslot_free"] = {0: (0, 0), 1: (0, 0), 2: (0, 0)}
        st["av_evac"] = 0
        r.wait("t", "v_m")
        r.wait("t", "sc_m")
        def _av_body(h, qc, pslot, psem, nk2):
            ngrp = (nk2 + 1) // 2
            for g2 in range(ngrp):
                gpar = g2 % 4
                ptsem = ("d_pt0", "d_pt1", "d_x2a", "d_x2b")[gpar]
                n_in = min(2, nk2 - g2 * 2)
                r.wait("y", "t_m", st.get(("pt_grp", gpar), 0))
                for i in range(n_in):
                    kc2 = g2 * 2 + i
                    tsl = gpar * 2 + i
                    r.do("y", lambda e, kc2=kc2, tsl=tsl, pslot=pslot:
                         e.dma_start(
                             out=att["ptile"][:, tsl * 512:(tsl + 1) * 512],
                             in_=p_hbm[pslot][0:512,
                                              kc2 * 128:(kc2 + 1) * 128],
                             transpose=True), ptsem)
                r.wait("t", ptsem)
                if g2 == 0:
                    r.wait("t", "v_m", st["av_evac"])
                for i in range(n_in):
                    kc2 = g2 * 2 + i
                    tsl = gpar * 2 + i
                    last = (kc2 == nk2 - 1)
                    r.do("t", lambda e, kc2=kc2, tsl=tsl, last=last:
                         e.matmul(
                             att["psd"][:, 0:512],
                             att["vsb"][:, kc2 * 128:(kc2 + 1) * 128],
                             att["ptile"][:, tsl * 512:(tsl + 1) * 512],
                             start=(kc2 == 0), stop=last),
                         "t_m" if (i == n_in - 1) else None)
                st[("pt_grp", gpar)] = r.ctr["t_m"]
            avm = r.ctr["t_m"]
            st["pslot_free"][pslot] = (r.ctr["d_pt0"], r.ctr["d_pt1"])
            r.wait("v", "t_m", avm)
            r.wait("v", "d_aw")
            r.do("v", lambda e: e.tensor_copy(att["avo"][:, 0:512],
                                              att["psd"][:, 0:512]), "v_m")
            st["av_evac"] = r.ctr["v_m"]
            r.wait("g", "v_m")
            r.do("g", lambda e, h=h, qc=qc: e.dma_start(
                out=attn_hbm[h * 128:(h + 1) * 128, qc * 512:(qc + 1) * 512],
                in_=att["avo"][:, 0:512]), "d_aw")



        pend_av = []

        def run_pend(keep=0):
            while len(pend_av) > keep:
                pend_av.pop(0)()

        for hl in range(4):
            h = hg * 4 + hl
            for qc in range(8):
                pslot = (hl * 8 + qc) % 3
                psem = ("d_p0", "d_p1", "d_p2")[pslot]
                nkc = qc + 1
                kfull = nkc * 512
                fr = st["pslot_free"][pslot]
                r.wait("g", "d_pt0", fr[0])
                r.wait("g", "d_pt1", fr[1])
                for qb4 in range(4):
                    qb = qc * 4 + qb4
                    sslot = qb4 % 2
                    ssb = att["ssb"][:, sslot * T:sslot * T + kfull]
                    pbf = att["pbf"][:, sslot * T:sslot * T + kfull]
                    r.wait("s", "sc_m", st.get(("ssb_rd", sslot), 0))
                    r.wait("v", "sc_m", st.get(("ssb_rd", sslot), 0))
                    for kc in range(nkc):
                        bank = (qb4 % 2) * 2 + kc % 2
                        sem_e, val_e = evb[bank]
                        r.wait("t", sem_e, val_e)
                        r.do("t", lambda e, kc=kc, bank=bank, hl=hl,
                             qb=qb: e.matmul(
                                 att["psg"][:, bank * 512:(bank + 1) * 512],
                                 att["qt"][:, hl * T + qb * 128:
                                           hl * T + qb * 128 + 128],
                                 att["kt"][:, kc * 512:(kc + 1) * 512],
                                 start=True, stop=True), "t_m")
                        if kc < nkc - 1:
                            r.wait("s", "t_m")
                            r.do("s", lambda e, kc=kc, bank=bank, ssb=ssb:
                                 e.activation(
                                     ssb[:, kc * 512:(kc + 1) * 512],
                                     att["psg"][:, bank * 512:(bank + 1) * 512],
                                     AF.Identity), "sc_m")
                            evb[bank] = ("sc_m", r.ctr["sc_m"])
                        else:
                            r.wait("v", "t_m")
                            r.do("v", lambda e, kc=kc, bank=bank, qb=qb,
                                 ssb=ssb: e.tensor_tensor(
                                     ssb[:, kc * 512:(kc + 1) * 512],
                                     att["psg"][:, bank * 512:(bank + 1) * 512],
                                     att["tri"][:, (qb % 4) * 512:
                                                (qb % 4 + 1) * 512],
                                     op=ALU.add), "v_m")
                            evb[bank] = ("v_m", r.ctr["v_m"])
                    r.wait("v", "sc_m")
                    r.wait("v", "v_m")
                    r.do("v", lambda e, ssb=ssb, qb4=qb4: e.reduce_max(
                        att["dn"][:, qb4 * 4:qb4 * 4 + 1],
                        ssb[:, :], axis=AX, negate=True), "v_m")
                    r.wait("v", "v_m")
                    r.do("v", lambda e, qb4=qb4: e.tensor_scalar(
                        att["dn"][:, qb4 * 4 + 1:qb4 * 4 + 2],
                        att["dn"][:, qb4 * 4:qb4 * 4 + 1],
                        float(1.0 / np.sqrt(np.float32(DK))), None,
                        op0=ALU.mult), "v_m")
                    r.wait("s", "v_m")
                    r.wait("s", psem)
                    r.do("s", lambda e, ssb=ssb, pbf=pbf, qb4=qb4:
                         e.activation(
                             pbf[:, :], ssb[:, :], AF.Exp,
                             bias=att["dn"][:, qb4 * 4 + 1:qb4 * 4 + 2],
                             scale=consts_sb[:, 2:3],
                             accum_out=att["dn"][:, qb4 * 4 + 2:qb4 * 4 + 3]),
                         "sc_m")
                    st[("ssb_rd", sslot)] = r.ctr["sc_m"]
                    r.wait("v", "sc_m")
                    r.do("v", lambda e, qb4=qb4: e.reciprocal(
                        att["dn"][:, qb4 * 4:qb4 * 4 + 1],
                        att["dn"][:, qb4 * 4 + 2:qb4 * 4 + 3]), "v_m")
                    r.wait("v", "v_m")
                    r.do("v", lambda e, pbf=pbf, qb4=qb4: e.tensor_scalar(
                        pbf[:, :], pbf[:, :],
                        att["dn"][:, qb4 * 4:qb4 * 4 + 1], None,
                        op0=ALU.mult), "v_m")
                    r.wait("g", "v_m")
                    r.do("g", lambda e, pbf=pbf, kfull=kfull, qb4=qb4,
                         pslot=pslot: e.dma_start(
                             out=p_hbm[pslot][qb4 * 128:(qb4 + 1) * 128,
                                              0:kfull],
                             in_=pbf[:, :]), psem)
                    st[("pbf_wr", sslot)] = r.ctr[psem]
                pwmark = r.ctr[psem]
                run_pend(keep=1)

                def av_now(h=h, qc=qc, pslot=pslot, psem=psem,
                           pwmark=pwmark):
                    self_qc = qc
                    nk2 = (self_qc + 1) * 4
                    r.wait("y", psem, pwmark)
                    _av_body(h, self_qc, pslot, psem, nk2)
                pend_av.append(av_now)
        run_pend()
        if False:
            for _unused in ():
                nk2 = (qc + 1) * 4
                r.wait("y", psem, pwmark)
    r.barrier()
    if DBG >= 2:
        r.do("y", lambda e: e.dma_start(out=dbg_out["attn"][:, :],
                                        in_=attn_hbm[:, :]), "d_res")
        r.barrier()
    for gd in aorder[::-1]:
        gd.__exit__(None, None, None)

    cset = {}
    corder = []
    def cal(name, shape, dt):
        gd = nc.sbuf_tensor(name + "_c", list(shape), dt)
        cset[name] = gd.__enter__()
        corder.append(gd)
        return cset[name]
    cal("asl", (128, KB * 512), BF16)
    cal("cwt", (128, KB * D), BF16)
    cal("x1t", (128, 4 * D), BF16)
    pgc = nc.psum_tensor("psgc", [128, 4 * TS], F32)
    cset["psg"] = pgc.__enter__(); corder.append(pgc)
    pdc = nc.psum_tensor("psdc", [128, 4 * 512], F32)
    cset["psd"] = pdc.__enter__(); corder.append(pdc)

    def mycopy1(e, lg):
        hoffc = (lg % 2) * 2048
        return e.dma_start(out=attnmy_hbm[:, :],
                           in_=attn_hbm[:, hoffc:hoffc + 2048])
    def mycopy2(e, lg):
        hoffr = (lg % 2) * 2048
        return e.dma_start(out=x1my_hbm[:, :],
                           in_=x1_hbm[hoffr:hoffr + 2048, :])
    def myswitch(e, lg):
        mycopy1(e, lg).then_inc(r.sems["d_att"], 16)
        mycopy2(e, lg).then_inc(r.sems["d_att"], 16)
    r.ev["y"].append(("switch", myswitch, None))
    r.ctr["d_att"] += 32
    r.do("y", lambda e: e.dma_start(out=cset["cwt"][:], in_=cwt_bf[:, :]),
         "d_att")
    cpm = r.ctr["d_att"]

    for tc in range(4):
        r.wait("y", "d_att", cpm)
        r.wait("y", "t_m", st.get("asl_free", 0))
        for k in range(KB):
            r.do("y", lambda e, tc=tc, k=k: e.dma_start(
                out=cset["asl"][:, k * 512:(k + 1) * 512],
                in_=attnmy_hbm[k * 128:(k + 1) * 128,
                               tc * 512:(tc + 1) * 512]), "d_att")
        r.wait("g", "v_m", st.get("x1t_free", 0))
        r.wait("g", "d_att", cpm)
        for t4 in range(4):
            rr = tc * 512 + t4 * TB
            r.do("g", lambda e, rr=rr, t4=t4: e.dma_start(
                out=cset["x1t"][:, t4 * D:(t4 + 1) * D],
                in_=x1my_hbm[rr:rr + TB, :]), "d_x")
        ldm_a = r.ctr["d_att"]
        ldm_x = r.ctr["d_x"]
        r.wait("t", "d_att", ldm_a)
        for t4 in range(4):
            ps = cset["psg"] if t4 % 2 == 0 else cset["psd"]
            r.wait("t", "v_m", st.get(("cp_evac", t4 % 2), 0))
            for k in range(KB):
                for n in range(4):
                    last = (k == KB - 1 and n == 3)
                    r.do("t", lambda e, t4=t4, k=k, n=n, ps=ps: e.matmul(
                        ps[:, n * 512:(n + 1) * 512],
                        cset["asl"][:, k * 512 + t4 * 128:k * 512 + t4 * 128 + 128],
                        cset["cwt"][:, k * D + n * 512:k * D + (n + 1) * 512],
                        start=(k == 0), stop=(k == KB - 1)),
                        "t_m" if last else None)
            r.wait("v", "t_m")
            r.wait("v", "d_x", ldm_x)
            r.do("v", lambda e, t4=t4, ps=ps: e.tensor_tensor(
                cset["x1t"][:, t4 * D:(t4 + 1) * D],
                cset["x1t"][:, t4 * D:(t4 + 1) * D],
                ps[:, 0:D], op=ALU.add), "v_m")
            st[("cp_evac", t4 % 2)] = r.ctr["v_m"]
            r.wait("y", "v_m")
            rr = tc * 512 + t4 * TB
            r.do("y", lambda e, rr=rr, t4=t4: e.dma_start(
                out=x2_hbm[rr:rr + TB, :],
                in_=cset["x1t"][:, t4 * D:(t4 + 1) * D]), "d_res")
        st["asl_free"] = r.ctr["t_m"]
        st["x1t_free"] = r.ctr["v_m"]
        r.wait("g", "d_res")

    r.barrier()
    if DBG >= 2:
        r.do("y", lambda e: e.dma_start(out=dbg_out["x2"][:, :],
                                        in_=x2_hbm[:, :]), "d_res")
        r.barrier()
    for gd in corder[::-1]:
        gd.__exit__(None, None, None)

    a2 = alloc_ffn_set("B")
    st["w1_cons"] = [r.ctr["t_m"], r.ctr["t_m"]]
    st["w2_cons"] = [r.ctr["t_m"]] * 5
    st["w2_pend"] = {}
    st["wb_cons"] = [r.ctr["v_m"], r.ctr["v_m"]]
    st["wb_rdmark"] = r.ctr["d_wb"]
    st["xnt_sb_free"] = r.ctr["t_m"]
    st["dn_evac"] = r.ctr["v_m"]
    st["xts_free_v"] = r.ctr["v_m"]
    st["xts_free_s"] = r.ctr["sc_m"]
    st["xts_free_d"] = r.ctr["d_res"]
    for ts in range(4):
        ffn_ts(a2, ts, "x2", "out")

    r.wait("y", "d_res")
    r.wait("g", "d_res")
    free_aset(a2)

    with nc.Block() as block:
        @block.sync
        def _(e):
            r.emit("y", e)

        @block.gpsimd
        def _(e):
            r.emit("g", e)

        @block.tensor
        def _(e):
            r.emit("t", e)

        @block.vector
        def _(e):
            r.emit("v", e)

        @block.scalar
        def _(e):
            r.emit("s", e)

    return nc



_CACHE = {}


def _host_inputs(L0_attn_x, L0_attn_q_w, L0_attn_kv_w, L0_attn_c_w,
                 L0_ffn_gate, L0_ffn_gate_up_proj, L0_ffn_down_proj):
    bf = ml_dtypes.bfloat16
    x = np.asarray(L0_attn_x, dtype=np.float32)
    qw = np.asarray(L0_attn_q_w, dtype=np.float32)
    kvw = np.asarray(L0_attn_kv_w, dtype=np.float32)
    cw = np.asarray(L0_attn_c_w, dtype=np.float32)
    gate = np.asarray(L0_ffn_gate, dtype=np.float32)
    gup = np.asarray(L0_ffn_gate_up_proj, dtype=np.float32)
    dwn = np.asarray(L0_ffn_down_proj, dtype=np.float32)

    perm = np.zeros(D, dtype=np.int64)
    for h in range(NH):
        b = h * DK
        perm[b:b + 64] = b + np.arange(0, DK, 2)
        perm[b + 64:b + DK] = b + np.arange(1, DK, 2)
    permk = np.zeros(NKV * DK, dtype=np.int64)
    for h in range(NKV):
        b = h * DK
        permk[b:b + 64] = b + np.arange(0, DK, 2)
        permk[b + 64:b + DK] = b + np.arange(1, DK, 2)

    qwt = np.ascontiguousarray(qw.T[:, perm])
    kvwt = kvw.T.copy()
    kvwt[:, :NKV * DK] = kvwt[:, :NKV * DK][:, permk]
    kvwt = np.ascontiguousarray(kvwt)
    cwt = np.ascontiguousarray(cw.T)
    gupf = np.ascontiguousarray(gup.reshape(E * D, 2 * FI))
    dwnf = np.ascontiguousarray(dwn.reshape(E * FI, D))

    pairs = np.arange(DK // 2, dtype=np.float32)
    freqs = (np.float32(1.0) /
             (np.float32(10000.0) ** (np.float32(2.0) * pairs / np.float32(DK))
              ).astype(np.float32)).astype(np.float32)
    pos = np.arange(T, dtype=np.float32)
    ang = (pos[:, None] * freqs[None, :]).astype(np.float32)
    cos = np.cos(ang).astype(np.float32)
    sin = np.sin(ang).astype(np.float32)
    cosq = np.ascontiguousarray(cos.T)
    sinq = np.ascontiguousarray(sin.T)
    cosk = cosq
    sink = sinq

    tri = np.zeros((128, 4 * 512), dtype=np.float32)
    for v in range(4):
        blkcol = np.arange(512) // 128
        for j in range(512):
            if blkcol[j] > v:
                tri[:, v * 512 + j] = -30000.0
            elif blkcol[j] == v:
                jj = j - v * 128
                tri[:, v * 512 + j] = np.where(np.arange(128) >= jj, 0.0,
                                               -30000.0)
    consts = np.zeros((128, 8), dtype=np.float32)
    consts[:, 0] = INVD
    consts[:, 1] = EPS
    consts[:, 2] = 1.0 / np.sqrt(np.float32(DK))
    iden32 = np.eye(128, dtype=np.float32)

    shared = dict(qwt=qwt, kvwt=kvwt, cwt=cwt, gatew=gate, gup=gupf, dwn=dwnf,
                  cosq=cosq, sinq=sinq, cosk=cosk, sink=sink, tri=tri,
                  consts=consts, iden32=iden32)
    in_maps = []
    for c in range(8):
        m = dict(shared)
        m["x0"] = np.ascontiguousarray(x[c // 2])
        in_maps.append(m)
    return in_maps


def kernel(L0_attn_x, L0_attn_q_w, L0_attn_kv_w, L0_attn_c_w,
           L0_ffn_gate, L0_ffn_gate_up_proj, L0_ffn_down_proj):
    if "nc" not in _CACHE:
        _CACHE["nc"] = build_program()
    nc = _CACHE["nc"]
    in_maps = _host_inputs(L0_attn_x, L0_attn_q_w, L0_attn_kv_w, L0_attn_c_w,
                           L0_ffn_gate, L0_ffn_gate_up_proj, L0_ffn_down_proj)
    res = run_bass_kernel_spmd(nc, in_maps, core_ids=list(range(8)))
    out = np.zeros((4, T, D), dtype=np.float32)
    for c in range(8):
        seq, half = c // 2, c % 2
        out[seq, half * 2048:(half + 1) * 2048, :] = res.results[c]["out"]
    return out
